# version 01
# baseline (speedup 1.0000x reference)
import numpy as np
import ml_dtypes
import concourse.bass as bass
import concourse.mybir as mybir

F32 = mybir.dt.float32
BF16 = mybir.dt.bfloat16
I8 = mybir.dt.int8
AF = mybir.ActivationFunctionType
ALU = mybir.AluOpType
AX = mybir.AxisListType
P = 128
T = 1024
D = 2048
NDC = D // P
NTC = T // P
EPS_LN = 1e-5
RSQRT_D = float(1.0 / np.sqrt(np.float64(2048.0)))
BF = ml_dtypes.bfloat16


class Sem:
    def __init__(self, nc, name):
        self.h = nc.alloc_semaphore(name)
        self.n = 0

    def inc(self, k):
        self.n += k
        return self.n

    @property
    def now(self):
        return (self, self.n)


class Emit:
    ENGINES = ("tensor", "vector", "scalar", "gpsimd", "sync")

    def __init__(self, nc):
        self.nc = nc
        self.ops = {e: [] for e in self.ENGINES}
        self._sems = {}

    def sem(self, name):
        if name not in self._sems:
            self._sems[name] = Sem(self.nc, name)
        return self._sems[name]

    SERIALIZE = ("vector", "scalar")

    def op(self, eng, fn, waits=(), inc=None, inc_n=1):
        waits = [w for w in waits if w is not None]
        if eng in self.SERIALIZE:
            last = getattr(self, "_last_op", {}).get(eng)
            if last is not None:
                waits = waits + [last]
            if inc is None:
                inc = self.sem(f"auto_{eng}")
                inc_n = 1
        if inc is not None:
            inc.inc(inc_n)
        if eng in self.SERIALIZE:
            if not hasattr(self, "_last_op"):
                self._last_op = {}
            self._last_op[eng] = (inc, inc.n)
        self.ops[eng].append({"fn": fn, "waits": waits, "inc": inc, "inc_n": inc_n})

    def dma(self, q, out, in_, sem, waits=()):
        self.op(q, lambda e, out=out, in_=in_: e.dma_start(out=out, in_=in_),
                waits, sem, 16)

    def mm(self, out, lhsT, rhs, start, stop, waits=(), inc=None):
        self.op("tensor",
                lambda e, o=out, l=lhsT, r=rhs, s=start, p=stop:
                e.matmul(o, l, r, start=s, stop=p), waits, inc)

    def trans(self, out, in_, waits=(), inc=None, ident=None):
        self.op("tensor",
                lambda e, o=out, i=in_, idn=ident:
                e.transpose(o, i, idn if idn is not None else self.ident[:, :]),
                waits, inc)

    def copy(self, eng, out, in_, waits=(), inc=None):
        self.op(eng, lambda e, o=out, i=in_: e.tensor_copy(o, i), waits, inc)

    def tt(self, eng, out, a, b, alu, waits=(), inc=None):
        self.op(eng, lambda e, o=out, a=a, b=b, u=alu: e.tensor_tensor(o, a, b, u),
                waits, inc)

    def ts(self, eng, out, in_, s1, s2, op0, op1=None, accum=None, waits=(),
           inc=None):
        def fn(e, o=out, i=in_, s1=s1, s2=s2, op0=op0, op1=op1, a=accum):
            if op1 is None:
                return e.tensor_scalar(o, i, s1, None, op0, accum_out=a)
            return e.tensor_scalar(o, i, s1, s2, op0, op1, accum_out=a)
        self.op(eng, fn, waits, inc)

    def stt(self, eng, out, in0, scalar, in1, op0, op1, waits=(), inc=None):
        self.op(eng,
                lambda e, o=out, i0=in0, s=scalar, i1=in1, a=op0, b=op1:
                e.scalar_tensor_tensor(o, i0, s, i1, a, b), waits, inc)

    def act(self, out, in_, func, bias=0.0, scale=1.0, accum=None, waits=(),
            inc=None):
        self.op("scalar",
                lambda e, o=out, i=in_, f=func, b=bias, s=scale, a=accum:
                e.activation(o, i, f, bias=b, scale=s, accum_out=a), waits, inc)

    def red(self, eng, out, in_, alu, negate=False, waits=(), inc=None):
        self.op(eng,
                lambda e, o=out, i=in_, u=alu, n=negate:
                e.tensor_reduce(o, i, AX.X, u, negate=n), waits, inc)

    def recip(self, out, in_, waits=(), inc=None):
        self.op("vector", lambda e, o=out, i=in_: e.reciprocal(o, i), waits, inc)

    def scan_cs(self, out, data, initial, waits=(), inc=None):
        self.op("vector",
                lambda e, o=out, d=data, i=initial:
                e.tensor_tensor_scan(o, d, d, i, ALU.add, ALU.bypass), waits, inc)

    def dbg(self, name, view, waits=()):
        if not getattr(self, "debug_dumps", False):
            return
        shp = [view.shape[0], int(np.prod(view.shape[1:]))]
        t = self.nc.dram_tensor(f"dbg_{name}", list(view.shape), view.dtype,
                                kind="ExternalOutput")
        self.dma("sync", t.ap(), view, self.sem("sto_dbg"), waits=waits)

    def build(self):
        with self.nc.Block() as block:
            for name in self.ENGINES:
                ops = self.ops[name]

                def body(eng, ops=ops):
                    for o in ops:
                        for (s, v) in o["waits"]:
                            eng.wait_ge(s.h, v)
                        inst = o["fn"](eng)
                        if o["inc"] is not None:
                            inst.then_inc(o["inc"].h, o["inc_n"])
                getattr(block, name)(body)


class Ring:

    def __init__(self, E, name, views):
        self.E = E
        self.views = list(views)
        self.n = len(views)
        self.sems = [E.sem(f"{name}_s{k}") for k in range(self.n)]
        self.freed = [None] * self.n
        self.i = 0

    def load(self, q, in_ap, waits=(), dst=None):
        k = self.i % self.n
        self.i += 1
        view = self.views[k]
        war = self.freed[k]
        sem = self.sems[k]
        self.E.dma(q, view if dst is None else dst(view), in_ap, sem,
                   waits=([war] if war else []) + list(waits))
        return view, (sem, sem.n)

    def set_free(self, view, sem_cnt):
        for k, v in enumerate(self.views):
            if v is view:
                self.freed[k] = sem_cnt
                return
        raise KeyError("view not in ring")


def alloc_consts(E):
    nc = E.nc
    E.ident_d = nc.inline_tensor(np.eye(P, dtype=BF), name="c_ident")
    E.onesr_d = nc.inline_tensor(np.ones((1, P), dtype=BF), name="c_onesr")
    E.onesc_d = nc.inline_tensor(np.ones((P, 1), dtype=BF), name="c_onesc")
    E.ident = nc.alloc_sbuf_tensor("ident", [P, P], BF16)
    E.ones_row = nc.alloc_sbuf_tensor("ones_row", [1, P], BF16)
    E.ones_col = nc.alloc_sbuf_tensor("ones_col", [P, 1], BF16)
    ld = E.sem("ldc")
    E.dma("gpsimd", E.ident[:, :], E.ident_d.ap(), ld)
    E.dma("gpsimd", E.ones_row[:, :], E.onesr_d.ap(), ld)
    E.dma("gpsimd", E.ones_col[:, :], E.onesc_d.ap(), ld)
    E.identf_d = nc.inline_tensor(np.eye(P, dtype=np.float32), name="c_identf")
    E.ident_f32 = nc.alloc_sbuf_tensor("ident_f32", [P, P], F32)
    E.dma("gpsimd", E.ident_f32[:, :], E.identf_d.ap(), ld)
    cv = np.array([[EPS_LN, 1e-9, 0.0, 0.0]], dtype=np.float32)
    E.cvals_d = nc.inline_tensor(cv, name="c_vals")
    E.cvals = nc.alloc_sbuf_tensor("cvals", [1, 4], F32)
    E.dma("gpsimd", E.cvals[:, :], E.cvals_d.ap(), ld)
    E.consts_at = ld.now


def emit_ln(E, src_tiles, out):
    pe, dve, act = E.sem("pe"), E.sem("dve"), E.sem("act")
    pb = E.pb
    sq = E.SCR[:, 0:T]
    scr = E.SCR[:, T:2 * T]
    sq_pe_at = None
    for dc in range(NDC):
        src, waits, free_sink = src_tiles[dc]()
        ocol = out[:, dc * T:(dc + 1) * T]
        w = list(waits)
        if dc == 0:
            w += [E.ln_war]
        E.copy("vector", ocol, src, waits=w, inc=dve)
        cast_at = dve.now
        free_sink(*cast_at)
        E.act(sq, ocol, AF.Square,
              waits=[cast_at] + ([sq_pe_at] if sq_pe_at else []), inc=act)
        sq_at = act.now
        for th in range(2):
            E.mm(pb[th][0:1, :], E.ones_col[:, :],
                 ocol[:, th * 512:(th + 1) * 512],
                 start=(dc == 0), stop=(dc == NDC - 1), waits=[cast_at, E.consts_at],
                 inc=pe if dc == NDC - 1 else None)
            E.mm(pb[2 + th][0:1, :], E.ones_col[:, :], sq[:, th * 512:(th + 1) * 512],
                 start=(dc == 0), stop=(dc == NDC - 1), waits=[sq_at], inc=pe)
        sq_pe_at = pe.now
    sums_at = pe.now
    dump_gate = None
    if getattr(E, "debug_dumps", False):
        sd = E.sem("sto_dbg")
        E.dbg(f"xpre{E._ln_ctr}", out[:, :], waits=[sums_at])
        dump_gate = (sd, sd.n)
    rbf = E.rows_bf
    for th in range(2):
        q = pb[2 + th][0:1, :]
        tmp = pb[4 + th][0:1, :]
        m_sb = E.rowm[0:1, 0:512]
        msq_sb = E.rowm[0:1, 512:1024]
        E.ts("vector", m_sb, pb[th][0:1, :], 1.0 / D, None, ALU.mult,
             waits=[sums_at, E.pb_war[4 + th]], inc=dve)
        E.ts("vector", q, q, 1.0 / D, None, ALU.mult, inc=dve)
        E.tt("vector", msq_sb, m_sb, m_sb, ALU.mult, inc=dve)
        E.tt("vector", q, q, msq_sb, ALU.subtract, inc=dve)
        E.act(tmp, q, AF.Sqrt, bias=E.cvals[0:1, 0:1],
              waits=[dve.now, E.consts_at], inc=act)
        E.recip(q, tmp, waits=[act.now], inc=dve)
        E.copy("vector", rbf[0:1, th * 512:(th + 1) * 512], q, inc=dve)
        E.tt("vector", rbf[0:1, T + th * 512:T + (th + 1) * 512], m_sb, q,
             ALU.mult, inc=dve)
    rows_at = dve.now
    for j, base in enumerate((0, T)):
        for th in range(2):
            pbb = pb[4 + th]
            E.mm(pbb[:, :], E.ones_row[:, :],
                 rbf[0:1, base + th * 512: base + (th + 1) * 512],
                 start=True, stop=True,
                 waits=[rows_at, E.consts_at, E.pb_war[4 + th]], inc=pe)
            E.copy("vector", E.RBPR[:, base + th * 512: base + (th + 1) * 512],
                   pbb[:, :], waits=[pe.now], inc=dve)
            E.pb_war[4 + th] = dve.now
    bcast_at = dve.now
    if getattr(E, "debug_dumps", False):
        E.dbg(f"rbf{E._ln_ctr}", E.rows_bf[0:1, :], waits=[rows_at])
        E.dbg(f"rbpr{E._ln_ctr}", E.RBPR[:, :], waits=[bcast_at])
        E._ln_ctr += 1
    for dc in range(NDC):
        ocol = out[:, dc * T:(dc + 1) * T]
        E.tt("vector", scr, ocol, E.RBPR[:, 0:T], ALU.mult,
             waits=([bcast_at] + ([dump_gate] if dc == 0 and dump_gate else []))
             if dc == 0 else (), inc=dve)
        E.tt("vector", ocol, scr, E.RBPR[:, T:2 * T], ALU.subtract, inc=dve)
    done = dve.now
    E.ln_war = done
    return done


def hbm_tile_src(E, hbm_ap, ring, q="gpsimd"):
    def make(dc):
        def get():
            view, at = ring.load(q, hbm_ap[dc * P:(dc + 1) * P, :])

            def free_sink(sem, cnt, view=view):
                ring.set_free(view, (sem, cnt))
            return view, [at], free_sink
        return get
    return [make(dc) for dc in range(NDC)]


def wslab_load(E, ring, wT_ap, col0, ncols, nchunk=NDC):
    srcap = wT_ap[:, col0:col0 + ncols].rearrange("(dc p) j -> p dc j", p=P)
    view, at = ring.load("gpsimd", srcap,
                         dst=lambda v: v[:, 0:nchunk * ncols].rearrange(
                             "p (dc j) -> p dc j", j=ncols))
    return view, at


def emit_l0_attn(E, own_xT, par_xT, wqkvT, bqkv_pk, bqkv_vrow, woutT, bout_pk,
                 x1_out):
    nc = E.nc
    alloc_consts(E)
    A = nc.alloc_sbuf_tensor("regA", [P, NDC * T], BF16)
    Bf = nc.alloc_sbuf_tensor("regB", [P, NDC * T], BF16)
    C = nc.alloc_sbuf_tensor("regC", [P, NDC * 2 * T], BF16)
    Dq = nc.alloc_sbuf_tensor("regD", [P, NDC * T], BF16)
    WR = [nc.alloc_sbuf_tensor(f"wr{i}", [P, NDC * 256], BF16) for i in range(2)]
    XR = [nc.alloc_sbuf_tensor(f"xr{i}", [P, T], F32) for i in range(2)]
    X1R = [nc.alloc_sbuf_tensor(f"x1r{i}", [P, 512], F32) for i in range(2)]
    E.SCR = nc.alloc_sbuf_tensor("scr2", [P, 2 * T], BF16)
    E.RBPR = nc.alloc_sbuf_tensor("rbpr", [P, 2 * T], BF16)
    E.rows_bf = nc.alloc_sbuf_tensor("rows_bf", [1, 2 * T], BF16)
    E.rowm = nc.alloc_sbuf_tensor("rowm", [1, T], BF16)
    bvb = nc.alloc_sbuf_tensor("bvb", [P, D], BF16)
    bq_sb = nc.alloc_sbuf_tensor("bq_sb", [P, 48], F32)
    bout_sb = nc.alloc_sbuf_tensor("bout_sb", [P, 16], F32)
    smax4 = nc.alloc_sbuf_tensor("smax4", [P, 4], F32)
    smax = nc.alloc_sbuf_tensor("smax", [P, 1], F32)
    sbias = nc.alloc_sbuf_tensor("sbias", [P, 1], F32)
    sden = nc.alloc_sbuf_tensor("sden", [P, 1], F32)
    srec = nc.alloc_sbuf_tensor("srec", [P, 1], F32)
    S = nc.alloc_psum_tensor("S", [P, 4 * 512], F32)
    pb45 = [nc.alloc_psum_tensor(f"pb{i}", [P, 512], F32) for i in (4, 5)]
    E.pb = [PBView(S, i) for i in range(4)] + [PBView(t, None) for t in pb45]
    ptb = [nc.alloc_psum_tensor(f"ptb{i}", [P, P], BF16) for i in range(2)]
    pb = E.pb
    E.ln_war = None
    E.pb_war = {4: None, 5: None}

    pe, dve, act = E.sem("pe"), E.sem("dve"), E.sem("act")
    ldb = E.sem("ldb")
    sto = [E.sem("sto0"), E.sem("sto1")]

    E.dma("gpsimd", bq_sb[:, :], bqkv_pk, ldb)
    E.dma("gpsimd", bout_sb[:, :], bout_pk, ldb)
    bias_at = ldb.now

    xring = Ring(E, "xr", [x[:, :] for x in XR])
    lnO_at = emit_ln(E, hbm_tile_src(E, own_xT, xring), A[:, :])
    E.dbg("lnO", A[:, :], waits=[lnO_at])
    lnP_at = emit_ln(E, hbm_tile_src(E, par_xT, xring), Bf[:, :])

    ldbv = E.sem("ldbv")
    bvrow = E.rows_bf
    E.dma("gpsimd", bvrow[0:1, 0:D], bqkv_vrow, ldbv, waits=[lnP_at])
    bvrow_at = ldbv.now
    for kb in range(4):
        pbb = pb[4 + kb % 2]
        E.mm(pbb.full(), E.ones_row[:, :], bvrow[0:1, kb * 512:(kb + 1) * 512],
             start=True, stop=True,
             waits=[bvrow_at, E.consts_at, E.pb_war[4 + kb % 2]], inc=pe)
        E.copy("vector", bvb[:, kb * 512:(kb + 1) * 512], pbb.full(),
               waits=[pe.now], inc=dve)
        E.pb_war[4 + kb % 2] = dve.now
    bvb_at = dve.now

    wring = Ring(E, "wr", [w[:, :] for w in WR])

    def proj_fm(out_cb, wT, wcol0, nwoc, spans, bias_col, wslab_cols=P):
        done = None
        for oc in range(nwoc):
            slab, slab_at = wslab_load(E, wring, wT, wcol0 + oc * wslab_cols,
                                       wslab_cols)
            tb_global = 0
            last_pe_for_slab = None
            for (region_at, region, col_fn) in spans:
                for tb in range(2):
                    pbx = 4 + tb_global % 2
                    pbb = pb[pbx]
                    for dc in range(NDC):
                        w = [slab_at]
                        if dc == 0:
                            w += [E.pb_war[pbx], region_at]
                        E.mm(pbb.full(),
                             slab[:, dc * wslab_cols:dc * wslab_cols + P],
                             col_fn(dc, tb),
                             start=(dc == 0), stop=(dc == NDC - 1), waits=w,
                             inc=pe if dc == NDC - 1 else None)
                    mm_at = pe.now
                    last_pe_for_slab = mm_at
                    E.act(out_cb(oc, tb_global), pbb.full(), AF.Identity,
                          bias=bias_col(oc), waits=[mm_at, bias_at], inc=act)
                    E.pb_war[pbx] = act.now
                    tb_global += 1
            wring.set_free(slab, last_pe_for_slab)
            done = act.now
        return done

    q_at = proj_fm(
        lambda oc, tb: Dq[:, oc * T + tb * 512:oc * T + (tb + 1) * 512],
        wqkvT, 0, NDC,
        [(lnO_at, A, lambda dc, tb: A[:, dc * T + tb * 512:dc * T + (tb + 1) * 512])],
        lambda oc: bq_sb[:, oc:oc + 1])

    E.dbg("q", Dq[:, :], waits=[q_at])
    k_at = proj_fm(
        lambda oc, tb: C[:, oc * 2 * T + tb * 512:oc * 2 * T + (tb + 1) * 512],
        wqkvT, D, NDC,
        [(lnO_at, A, lambda dc, tb: A[:, dc * T + tb * 512:dc * T + (tb + 1) * 512]),
         (lnP_at, Bf, lambda dc, tb: Bf[:, dc * T + tb * 512:dc * T + (tb + 1) * 512])],
        lambda oc: bq_sb[:, 16 + oc:17 + oc])

    E.dbg("k", C[:, :], waits=[k_at])
    probs = E.RBPR
    prev_exp_at = None
    ptb_war = {0: None, 1: None}
    pt_done = []
    for qc in range(NTC):
        for kb in range(4):
            for dc in range(NDC):
                w = []
                if dc == 0:
                    w = [q_at, k_at]
                    if prev_exp_at is not None:
                        w.append(prev_exp_at)
                E.mm(S[:, kb * 512:(kb + 1) * 512],
                     Dq[:, dc * T + qc * P:dc * T + (qc + 1) * P],
                     C[:, dc * 2 * T + kb * 512:dc * 2 * T + (kb + 1) * 512],
                     start=(dc == 0), stop=(dc == NDC - 1), waits=w,
                     inc=pe if dc == NDC - 1 else None)
        s_at = pe.now
        for kb in range(4):
            E.red("vector", smax4[:, kb:kb + 1], S[:, kb * 512:(kb + 1) * 512],
                  ALU.max, waits=[s_at] if kb == 0 else (), inc=dve)
        E.red("vector", smax[:, 0:1], smax4[:, 0:4], ALU.max, inc=dve)
        E.ts("vector", sbias[:, 0:1], smax[:, 0:1], -RSQRT_D, None, ALU.mult,
             inc=dve)
        bias_ready = dve.now
        E.act(probs[:, 0:2 * T], S[:, :], AF.Exp, bias=sbias[:, 0:1],
              scale=RSQRT_D,
              waits=[bias_ready, s_at] + ([lnP_at] if qc == 0 else []) +
              ([ptb_war[0], ptb_war[1]] if qc else []), inc=act)
        exp_at = act.now
        prev_exp_at = exp_at
        E.red("vector", sden[:, 0:1], probs[:, 0:2 * T], ALU.add,
              waits=[exp_at], inc=dve)
        E.recip(srec[:, 0:1], sden[:, 0:1], waits=[dve.now], inc=dve)
        rec_at = dve.now
        E.ts("vector", probs[:, 0:2 * T], probs[:, 0:2 * T], srec[:, 0:1], None,
             ALU.mult, waits=[rec_at], inc=dve)
        norm_at = dve.now
        for kc in range(2 * NTC):
            pt = ptb[kc % 2]
            E.trans(pt[:, :], probs[:, kc * P:(kc + 1) * P],
                    waits=[norm_at, E.consts_at] +
                    ([ptb_war[kc % 2]] if ptb_war[kc % 2] else []), inc=pe)
            E.copy("vector", Dq[:, kc * T + qc * P: kc * T + (qc + 1) * P],
                   pt[:, :], waits=[pe.now], inc=dve)
            ptb_war[kc % 2] = dve.now
        if qc == 0:
            E.dbg("probs0", probs[:, 0:2 * T], waits=[norm_at])
        pt_done.append(dve.now)
    pt_all = pt_done[-1]
    E.dbg("pt", Dq[:, :], waits=[pt_all])

    v_done = None
    for hb in range(8):
        slab, slab_at = wslab_load(E, wring, wqkvT, 2 * D + hb * 256, 256)
        last_pe = None
        for tc in range(2 * NTC):
            reg, reg_at = (A, lnO_at) if tc < NTC else (Bf, lnP_at)
            tloc = tc % NTC
            pbx = 4 + tc % 2
            pbb = pb[pbx]
            for dc in range(NDC):
                w = [slab_at]
                if dc == 0:
                    w += [E.pb_war[pbx], reg_at]
                E.mm(pbb.part(0, 256),
                     reg[:, dc * T + tloc * P:dc * T + (tloc + 1) * P],
                     slab[:, dc * 256:(dc + 1) * 256],
                     start=(dc == 0), stop=(dc == NDC - 1), waits=w,
                     inc=pe if dc == NDC - 1 else None)
            mm_at = pe.now
            last_pe = mm_at
            E.tt("vector", C[:, tc * D + hb * 256:tc * D + (hb + 1) * 256],
                 pbb.part(0, 256), bvb[:, hb * 256:(hb + 1) * 256], ALU.add,
                 waits=[mm_at, bvb_at, pt_all], inc=dve)
            E.pb_war[pbx] = dve.now
        wring.set_free(slab, last_pe)
        v_done = dve.now

    E.dbg("v", C[:, :], waits=[v_done])
    ctx_done = None
    for hvc in range(NDC):
        for qh in range(2):
            pbx = 4 + (hvc * 2 + qh) % 2
            pbb = pb[pbx]
            for kc in range(2 * NTC):
                w = [v_done, pt_all]
                if kc == 0:
                    w += [E.pb_war[pbx]]
                E.mm(pbb.full(),
                     C[:, kc * D + hvc * P:kc * D + (hvc + 1) * P],
                     Dq[:, kc * T + qh * 512:kc * T + (qh + 1) * 512],
                     start=(kc == 0), stop=(kc == 2 * NTC - 1),
                     waits=w if kc == 0 else (),
                     inc=pe if kc == 2 * NTC - 1 else None)
            mm_at = pe.now
            E.act(A[:, hvc * T + qh * 512:hvc * T + (qh + 1) * 512], pbb.full(),
                  AF.Copy, waits=[mm_at], inc=act)
            E.pb_war[pbx] = act.now
            ctx_done = act.now

    E.dbg("ctx", A[:, :], waits=[ctx_done])
    x1r_views = [x[:, :] for x in X1R]
    x1r_freed = [None, None]
    x1r_i = [0]
    for oc in range(NDC):
        slab, slab_at = wslab_load(E, wring, woutT, oc * P, P)
        xv, x0_at = xring.load("gpsimd", own_xT[oc * P:(oc + 1) * P, :])
        last_pe = None
        for th in range(2):
            pbx = 4 + th % 2
            pbb = pb[pbx]
            for hvc in range(NDC):
                w = [slab_at, ctx_done]
                if hvc == 0:
                    w += [E.pb_war[pbx]]
                E.mm(pbb.full(), slab[:, hvc * P:(hvc + 1) * P],
                     A[:, hvc * T + th * 512:hvc * T + (th + 1) * 512],
                     start=(hvc == 0), stop=(hvc == NDC - 1),
                     waits=w if hvc == 0 else [slab_at],
                     inc=pe if hvc == NDC - 1 else None)
            mm_at = pe.now
            last_pe = mm_at
            k = x1r_i[0] % 2
            x1r_i[0] += 1
            ov = x1r_views[k]
            owar = x1r_freed[k]
            E.stt("vector", ov, pbb.full(), bout_sb[:, oc:oc + 1],
                  xv[:, th * 512:(th + 1) * 512], ALU.add, ALU.add,
                  waits=[mm_at, x0_at, bias_at] + ([owar] if owar else []),
                  inc=dve)
            E.pb_war[pbx] = dve.now
            E.dma("sync", x1_out[oc * P:(oc + 1) * P, th * 512:(th + 1) * 512],
                  ov, sto[k], waits=[dve.now])
            x1r_freed[k] = (sto[k], sto[k].n)
        xring.set_free(xv, dve.now)
        wring.set_free(slab, last_pe)
    E.op("sync", lambda e: e.nop(), waits=[sto[0].now, sto[1].now])


class PBView:

    def __init__(self, tensor, slice_idx):
        self.t = tensor
        self.i = slice_idx

    def __getitem__(self, key):
        if self.i is None:
            return self.t[key]
        rows, cols = key
        c0, c1 = cols.start or 0, cols.stop if cols.stop is not None else 512
        return self.t[rows, self.i * 512 + c0:self.i * 512 + c1]

    def full(self):
        return self[0:P, 0:512]

    def part(self, c0, c1):
        return self[0:P, c0:c1]


def emit_l0_ffn(E, x1_hbm, rwT, w1_ap, w2_ap, w3_ap, x2_out):
    nc = E.nc
    alloc_consts(E)
    X1 = nc.alloc_sbuf_tensor("X1", [P, NDC * T], F32)
    LNF = nc.alloc_sbuf_tensor("LNF", [P, NDC * T], BF16)
    GB = nc.alloc_sbuf_tensor("GB", [P, 8 * T], BF16)
    VB = nc.alloc_sbuf_tensor("VB", [P, 8 * T], BF16)
    WR = [nc.alloc_sbuf_tensor(f"wr{i}", [P, 16 * P], BF16) for i in range(2)]
    E.SCR = nc.alloc_sbuf_tensor("scr2", [P, 2 * T], BF16)
    E.RBPR = nc.alloc_sbuf_tensor("rbpr", [P, 2 * T], BF16)
    E.rows_bf = nc.alloc_sbuf_tensor("rows_bf", [1, 2 * T], BF16)
    E.rowm = nc.alloc_sbuf_tensor("rowm", [1, T], BF16)
    rw32 = nc.alloc_sbuf_tensor("rw32", [P, 16 * 8], F32)
    rwhl = nc.alloc_sbuf_tensor("rwhl", [P, 2 * 16 * 8], BF16)
    rwsn = nc.alloc_sbuf_tensor("rwsn", [8, 1], F32)
    LG = nc.alloc_sbuf_tensor("LG", [8, T], F32)
    Ltok = nc.alloc_sbuf_tensor("Ltok", [P, 8 * 8], F32)
    wtok = nc.alloc_sbuf_tensor("wtok", [P, 8 * 8], F32)
    msk = nc.alloc_sbuf_tensor("msk", [P, 8], F32)
    m1r = nc.alloc_sbuf_tensor("m1r", [P, 1], F32)
    m2r = nc.alloc_sbuf_tensor("m2r", [P, 1], F32)
    p1r = nc.alloc_sbuf_tensor("p1r", [P, 1], F32)
    p2r = nc.alloc_sbuf_tensor("p2r", [P, 1], F32)
    eq1 = nc.alloc_sbuf_tensor("eq1", [P, 8], F32)
    wfm8 = nc.alloc_sbuf_tensor("wfm8", [8, T], F32)
    wfm8b = nc.alloc_sbuf_tensor("wfm8b", [8, T], BF16)
    wrows = nc.alloc_sbuf_tensor("wrows", [1, 8 * T], BF16)
    WB = nc.alloc_sbuf_tensor("WB", [P, T], BF16)
    S = nc.alloc_psum_tensor("S", [P, 4 * 512], F32)
    pb45 = [nc.alloc_psum_tensor(f"pb{i}", [P, 512], F32) for i in (4, 5)]
    E.pb = [PBView(S, i) for i in range(4)] + [PBView(t, None) for t in pb45]
    pb = E.pb
    E.ln_war = None
    E.pb_war = {4: None, 5: None}

    pe, dve, act = E.sem("pe"), E.sem("dve"), E.sem("act")
    ldx1, sto = E.sem("ldx1"), E.sem("sto")

    for dc in range(NDC):
        E.dma("gpsimd", X1[:, dc * T:(dc + 1) * T], x1_hbm[dc * P:(dc + 1) * P, :],
              ldx1)
    x1_at = ldx1.now

    def src(dc):
        def get():
            return X1[:, dc * T:(dc + 1) * T], [x1_at], (lambda s, c: None)
        return get
    ln_at = emit_ln(E, [src(dc) for dc in range(NDC)], LNF[:, :])

    ldrw = E.sem("ldrw")
    for dc in range(NDC):
        E.dma("gpsimd", rw32[:, dc * 8:(dc + 1) * 8],
              rwT[dc * P:(dc + 1) * P, :], ldrw)
    rw_at = ldrw.now
    E.copy("vector", rwhl[:, 0:128], rw32[:, :], waits=[rw_at], inc=dve)
    E.tt("vector", rwhl[:, 128:256], rw32[:, :], rwhl[:, 0:128], ALU.subtract,
         waits=[dve.now], inc=dve)
    rwhl_at = dve.now
    ps8 = S[0:8, 1024:1025]
    for j, col0 in enumerate((0, 128)):
        for dc in range(NDC):
            E.mm(ps8, rwhl[:, col0 + dc * 8:col0 + (dc + 1) * 8],
                 E.ones_col[:, :], start=(j == 0 and dc == 0),
                 stop=(j == 1 and dc == NDC - 1),
                 waits=[rwhl_at, ln_at] if j == 0 and dc == 0 else (),
                 inc=pe if (j == 1 and dc == NDC - 1) else None)
    E.ts("vector", rwsn[:, :], ps8, -1.0, None, ALU.mult, waits=[pe.now],
         inc=dve)
    rwsn_at = dve.now
    PL = S[0:8, 0:1024]
    hi = E.SCR[:, 0:T]
    lo = E.SCR[:, T:2 * T]
    prev_mm = None
    for dc in range(NDC):
        E.copy("vector", hi, X1[:, dc * T:(dc + 1) * T],
               waits=[prev_mm, ln_at], inc=dve)
        E.tt("vector", lo, X1[:, dc * T:(dc + 1) * T], hi, ALU.subtract,
             inc=dve)
        cast_at = dve.now
        for si, (wcol, xbuf) in enumerate(((0, hi), (0, lo), (128, hi))):
            for th in range(2):
                E.mm(PL[0:8, th * 512:(th + 1) * 512],
                     rwhl[:, wcol + dc * 8:wcol + (dc + 1) * 8],
                     xbuf[:, th * 512:(th + 1) * 512],
                     start=(dc == 0 and si == 0),
                     stop=(dc == NDC - 1 and si == 2),
                     waits=[cast_at] if si == 0 and th == 0 else (),
                     inc=pe if (si == 2 and th == 1) else None)
        prev_mm = pe.now
    raw_at = pe.now
    E.tt("vector", LG[0:8, :], PL, E.RBPR[0:8, 0:T], ALU.mult,
         waits=[raw_at], inc=dve)
    E.stt("vector", LG[0:8, :], E.RBPR[0:8, T:2 * T], rwsn[:, :], LG[0:8, :],
          ALU.mult, ALU.add, waits=[dve.now, rwsn_at], inc=dve)
    lg_at = dve.now
    tr_war = {4: E.pb_war[4], 5: E.pb_war[5]}
    for tc in range(NTC):
        pbx = 4 + tc % 2
        E.trans(pb[pbx].part(0, 8), LG[0:8, tc * P:(tc + 1) * P],
                waits=[lg_at, E.consts_at, tr_war[pbx]], inc=pe,
                ident=E.ident_f32[0:8, 0:8])
        E.copy("vector", Ltok[:, tc * 8:(tc + 1) * 8], pb[pbx].part(0, 8),
               waits=[pe.now], inc=dve)
        tr_war[pbx] = dve.now
    dbgrow = nc.alloc_sbuf_tensor("dbgrow", [P, NTC * 4], F32)
    dbge = nc.alloc_sbuf_tensor("dbge", [P, NTC * 16], F32)
    for tc in range(NTC):
        lt = Ltok[:, tc * 8:(tc + 1) * 8]
        wt = wtok[:, tc * 8:(tc + 1) * 8]
        E.red("vector", m1r[:, :], lt, ALU.max, waits=[tr_war[4], tr_war[5]]
              if tc == 0 else (), inc=dve)
        m1_at = dve.now
        E.ts("vector", msk[:, :], lt, m1r[:, :], -1e30, ALU.is_equal, ALU.mult,
             waits=[m1_at], inc=dve)
        E.tt("vector", msk[:, :], msk[:, :], lt, ALU.add, inc=dve)
        msk_at = dve.now
        E.red("vector", m2r[:, :], msk[:, :], ALU.max, waits=[msk_at], inc=dve)
        m2_at = dve.now
        E.tt("vector", p1r[:, :], m1r[:, :], m2r[:, :], ALU.subtract,
             waits=[m2_at], inc=dve)
        E.act(p1r[:, :], p1r[:, :], AF.Sigmoid, waits=[dve.now], inc=act)
        sig_at = act.now
        E.ts("vector", p2r[:, :], p1r[:, :], -1.0, 1.0, ALU.mult, ALU.add,
             waits=[sig_at], inc=dve)
        p_at = dve.now
        E.ts("vector", eq1[:, :], lt, m1r[:, :], None, ALU.is_equal,
             waits=[p_at], inc=dve)
        E.ts("vector", eq1[:, :], eq1[:, :], p1r[:, :], None, ALU.mult,
             waits=[dve.now], inc=dve)
        E.ts("vector", wt, msk[:, :], m2r[:, :], None, ALU.is_equal,
             waits=[dve.now], inc=dve)
        E.ts("vector", wt, wt, p2r[:, :], None, ALU.mult, waits=[dve.now],
             inc=dve)
        E.tt("vector", wt, wt, eq1[:, :], ALU.add, inc=dve)
        if getattr(E, "debug_dumps", False):
            E.copy("vector", dbgrow[:, tc * 4:tc * 4 + 1], m1r[:, :], inc=dve)
            E.copy("vector", dbgrow[:, tc * 4 + 1:tc * 4 + 2], m2r[:, :], inc=dve)
            E.copy("vector", dbgrow[:, tc * 4 + 2:tc * 4 + 3], p1r[:, :], inc=dve)
            E.copy("vector", dbgrow[:, tc * 4 + 3:tc * 4 + 4], p2r[:, :], inc=dve)
            E.copy("vector", dbge[:, tc * 16:tc * 16 + 8], eq1[:, :], inc=dve)
            E.copy("vector", dbge[:, tc * 16 + 8:tc * 16 + 16], msk[:, :], inc=dve)
    w_at = dve.now
    E.dbg("rows", dbgrow[:, :], waits=[w_at])
    E.dbg("eqm", dbge[:, :], waits=[w_at])
    E.dbg("LG", LG[0:8, :], waits=[lg_at])
    E.dbg("Ltok", Ltok[:, :], waits=[w_at])
    E.dbg("wtok", wtok[:, :], waits=[w_at])
    for tc in range(NTC):
        pbx = 4 + tc % 2
        E.trans(pb[pbx][0:8, 0:P], wtok[:, tc * 8:(tc + 1) * 8],
                waits=[w_at, tr_war[pbx]], inc=pe,
                ident=E.ident_f32[0:P, 0:P])
        E.copy("vector", wfm8[0:8, tc * P:(tc + 1) * P], pb[pbx][0:8, 0:P],
               waits=[pe.now], inc=dve)
        tr_war[pbx] = dve.now
    E.copy("vector", wfm8b[0:8, :], wfm8[0:8, :], waits=[dve.now], inc=dve)
    wfm_at = dve.now
    ldwr = E.sem("ldwr")
    for e in range(8):
        E.dma("gpsimd", wrows[0:1, e * T:(e + 1) * T], wfm8b[e:e + 1, :], ldwr,
              waits=[wfm_at])
    wrows_at = ldwr.now
    E.dbg("wrows", wrows[0:1, :], waits=[wrows_at])

    wring = Ring(E, "wrf", [w[:, :] for w in WR])
    x1_written = None
    for e in range(8):
        for th in range(2):
            pbx = 4 + th
            E.mm(pb[pbx][0:P, 0:512], E.ones_row[:, :],
                 wrows[0:1, e * T + th * 512:e * T + (th + 1) * 512],
                 start=True, stop=True,
                 waits=[wrows_at, E.pb_war[pbx]], inc=pe)
            E.copy("vector", WB[:, th * 512:(th + 1) * 512], pb[pbx][0:P, 0:512],
                   waits=[pe.now], inc=dve)
            E.pb_war[pbx] = dve.now
        wb_at = dve.now
        for (wap, dest, fn_act) in ((w1_ap, GB, AF.Silu), (w2_ap, VB, AF.Identity)):
            wap_e = wap[e]
            for fc in range(8):
                slab, slab_at = wslab_load(E, wring, wap_e, fc * P, P)
                last_pe = None
                for th in range(2):
                    pbx = 4 + th
                    for dc in range(NDC):
                        w = [slab_at]
                        if dc == 0:
                            w += [E.pb_war[pbx], ln_at]
                        E.mm(pb[pbx][0:P, 0:512],
                             slab[:, dc * P:(dc + 1) * P],
                             LNF[:, dc * T + th * 512:dc * T + (th + 1) * 512],
                             start=(dc == 0), stop=(dc == NDC - 1), waits=w,
                             inc=pe if dc == NDC - 1 else None)
                    mm_at = pe.now
                    last_pe = mm_at
                    E.act(dest[:, fc * T + th * 512:fc * T + (th + 1) * 512],
                          pb[pbx][0:P, 0:512], fn_act, waits=[mm_at], inc=act)
                    E.pb_war[pbx] = act.now
                wring.set_free(slab, last_pe)
        gv_at = act.now
        for fc in range(8):
            E.tt("vector", GB[:, fc * T:(fc + 1) * T], GB[:, fc * T:(fc + 1) * T],
                 VB[:, fc * T:(fc + 1) * T], ALU.mult,
                 waits=[gv_at, wb_at] if fc == 0 else (), inc=dve)
            E.tt("vector", GB[:, fc * T:(fc + 1) * T], GB[:, fc * T:(fc + 1) * T],
                 WB[:, :], ALU.mult, inc=dve)
        h_at = dve.now
        for oc in range(NDC):
            slab, slab_at = wslab_load(E, wring, w3_ap[e], oc * P, P, nchunk=8)
            last_pe = None
            for th in range(2):
                pbx = 4 + th
                for fc in range(8):
                    w = [slab_at, h_at]
                    if fc == 0:
                        w += [E.pb_war[pbx]]
                    E.mm(pb[pbx][0:P, 0:512], slab[:, fc * P:(fc + 1) * P],
                         GB[:, fc * T + th * 512:fc * T + (th + 1) * 512],
                         start=(fc == 0), stop=(fc == 7), waits=w,
                         inc=pe if fc == 7 else None)
                mm_at = pe.now
                last_pe = mm_at
                xsl = X1[:, oc * T + th * 512:oc * T + (th + 1) * 512]
                E.tt("vector", xsl, xsl, pb[pbx][0:P, 0:512], ALU.add,
                     waits=[mm_at], inc=dve)
                E.pb_war[pbx] = dve.now
            wring.set_free(slab, last_pe)
        x1_written = dve.now
    for dc in range(NDC):
        E.dma("sync", x2_out[dc * P:(dc + 1) * P, :], X1[:, dc * T:(dc + 1) * T],
              sto, waits=[x1_written])
    E.op("sync", lambda e: e.nop(), waits=[sto.now])


def emit_l1a_part1(E, x2_hbm, wqkvT, phiq_out, kvc_out, kc_out, tot_out):
    nc = E.nc
    alloc_consts(E)
    X = nc.alloc_sbuf_tensor("X", [P, NDC * T], F32)
    LNF = nc.alloc_sbuf_tensor("LNF", [P, NDC * T], BF16)
    WR = [nc.alloc_sbuf_tensor(f"wr{i}", [P, 16 * P], BF16) for i in range(2)]
    E.SCR = nc.alloc_sbuf_tensor("scr2", [P, 2 * T], BF16)
    E.RBPR = nc.alloc_sbuf_tensor("rbpr", [P, 2 * T], BF16)
    E.rows_bf = nc.alloc_sbuf_tensor("rows_bf", [1, 2 * T], BF16)
    E.rowm = nc.alloc_sbuf_tensor("rowm", [1, T], BF16)
    phk = nc.alloc_sbuf_tensor("phk", [P, T], BF16)
    vv = nc.alloc_sbuf_tensor("vv", [P, T], BF16)
    kv = nc.alloc_sbuf_tensor("kv", [P, T], BF16)
    sc1 = nc.alloc_sbuf_tensor("sc1", [P, T], F32)
    sc2 = nc.alloc_sbuf_tensor("sc2", [P, T], BF16)
    OUTR = [nc.alloc_sbuf_tensor(f"or{i}", [P, T], F32) for i in range(4)]
    PQR = [nc.alloc_sbuf_tensor(f"pq{i}", [P, T], BF16) for i in range(2)]
    TOT = nc.alloc_sbuf_tensor("TOT", [P, 32], F32)
    S = nc.alloc_psum_tensor("S", [P, 4 * 512], F32)
    pb45 = [nc.alloc_psum_tensor(f"pb{i}", [P, 512], F32) for i in (4, 5)]
    E.pb = [PBView(S, i) for i in range(4)] + [PBView(t, None) for t in pb45]
    pb = E.pb
    E.ln_war = None
    E.pb_war = {4: None, 5: None}
    pe, dve, act = E.sem("pe"), E.sem("dve"), E.sem("act")
    ldx1 = E.sem("ldx1")
    spq = [E.sem("spq0"), E.sem("spq1")]
    sor = [E.sem(f"sor{i}") for i in range(4)]
    stt = E.sem("stt")

    for dc in range(NDC):
        E.dma("gpsimd", X[:, dc * T:(dc + 1) * T], x2_hbm[dc * P:(dc + 1) * P, :],
              ldx1)
    x_at = ldx1.now

    def src(dc):
        def get():
            return X[:, dc * T:(dc + 1) * T], [x_at], (lambda s, c: None)
        return get
    ln_at = emit_ln(E, [src(dc) for dc in range(NDC)], LNF[:, :])

    wring = Ring(E, "wr", [w[:, :] for w in WR])
    pqring_i = [0]
    pq_freed = [None, None]
    or_i = [0]
    or_freed = [None] * 4
    tot_written = None

    def proj_chunk(wcol, epi):
        slab, slab_at = wslab_load(E, wring, wqkvT, wcol, P)
        last_pe = None
        for th in range(2):
            pbx = 4 + th
            pbb = pb[pbx]
            for dc in range(NDC):
                w = [slab_at]
                if dc == 0:
                    w += [E.pb_war[pbx], ln_at]
                E.mm(pbb.full(), slab[:, dc * P:(dc + 1) * P],
                     LNF[:, dc * T + th * 512:dc * T + (th + 1) * 512],
                     start=(dc == 0), stop=(dc == NDC - 1), waits=w,
                     inc=pe if dc == NDC - 1 else None)
            mm_at = pe.now
            last_pe = mm_at
            epi(th, pbb, mm_at)
        wring.set_free(slab, last_pe)

    def phi_epi(dest):
        def epi(th, pbb, mm_at):
            sl = slice(th * 512, (th + 1) * 512)
            E.ts("vector", sc1[:, sl], pbb.full(), 0.0, None, ALU.min,
                 waits=[mm_at], inc=dve)
            E.act(sc2[:, sl], sc1[:, sl], AF.Exp, waits=[dve.now], inc=act)
            E.stt("vector", dest[:, sl], pbb.full(), 1.0, sc2[:, sl],
                  ALU.add, ALU.max, waits=[act.now, mm_at], inc=dve)
            E.pb_war[4 + th] = dve.now
        return epi

    for j in range(NDC):
        k2 = pqring_i[0] % 2
        pqring_i[0] += 1
        pqv = PQR[k2][:, :]
        if pq_freed[k2] is not None:
            pass
        def q_epi(th, pbb, mm_at, pqv=pqv, k2=k2):
            sl = slice(th * 512, (th + 1) * 512)
            E.ts("vector", sc1[:, sl], pbb.full(), 0.0, None, ALU.min,
                 waits=[mm_at], inc=dve)
            E.act(sc2[:, sl], sc1[:, sl], AF.Exp, waits=[dve.now], inc=act)
            E.stt("vector", pqv[:, sl], pbb.full(), 1.0, sc2[:, sl],
                  ALU.add, ALU.max,
                  waits=[act.now, mm_at] + ([pq_freed[k2]] if th == 0 and pq_freed[k2] else []),
                  inc=dve)
            E.pb_war[4 + th] = dve.now
        proj_chunk(j * P, q_epi)
        E.dma("sync", phiq_out[j * P:(j + 1) * P, :], pqv, spq[k2],
              waits=[dve.now])
        pq_freed[k2] = (spq[k2], spq[k2].n)
        proj_chunk(D + j * P, phi_epi(phk))
        phk_at = dve.now
        def v_epi(th, pbb, mm_at):
            sl = slice(th * 512, (th + 1) * 512)
            E.copy("vector", vv[:, sl], pbb.full(), waits=[mm_at], inc=dve)
            E.pb_war[4 + th] = dve.now
        proj_chunk(2 * D + j * P, v_epi)
        v_at = dve.now
        E.tt("vector", kv[:, :], phk[:, :], vv[:, :], ALU.mult,
             waits=[phk_at, v_at], inc=dve)
        k4 = or_i[0] % 4
        or_i[0] += 1
        kvcv = OUTR[k4][:, :]
        E.scan_cs(kvcv, kv[:, :], 0.0,
                  waits=[dve.now] + ([or_freed[k4]] if or_freed[k4] else []),
                  inc=dve)
        E.copy("vector", TOT[:, j:j + 1], kvcv[:, T - 1:T], inc=dve)
        E.dma("sync", kvc_out[j * P:(j + 1) * P, :], kvcv, sor[k4],
              waits=[dve.now])
        or_freed[k4] = (sor[k4], sor[k4].n)
        k4 = or_i[0] % 4
        or_i[0] += 1
        kcv = OUTR[k4][:, :]
        E.scan_cs(kcv, phk[:, :], 0.0,
                  waits=[dve.now] + ([or_freed[k4]] if or_freed[k4] else []),
                  inc=dve)
        E.copy("vector", TOT[:, 16 + j:17 + j], kcv[:, T - 1:T], inc=dve)
        E.dma("sync", kc_out[j * P:(j + 1) * P, :], kcv, sor[k4],
              waits=[dve.now])
        or_freed[k4] = (sor[k4], sor[k4].n)
        tot_written = dve.now
    E.dma("sync", tot_out, TOT[:, :], stt, waits=[tot_written])
    E.op("sync", lambda e: e.nop(),
         waits=[stt.now, spq[0].now, spq[1].now] + [s.now for s in sor])


def emit_l1a2_l1f(E, x2_hbm, phiq_hbm, kvc_hbm, kc_hbm, ptot_hbm, woutT,
                  wg_ap, wu_ap, wd_ap, sg_pk, su_pk, sd_pk, x4_out):
    nc = E.nc
    alloc_consts(E)
    X = nc.alloc_sbuf_tensor("X", [P, NDC * T], F32)
    LNF = nc.alloc_sbuf_tensor("LNF", [P, NDC * T], BF16)
    OP = LNF
    GU = nc.alloc_sbuf_tensor("GU", [P, 16 * T], BF16)
    WR = [nc.alloc_sbuf_tensor(f"wr{i}", [P, 16 * P], BF16) for i in range(2)]
    E.SCR = nc.alloc_sbuf_tensor("scr2", [P, 2 * T], BF16)
    E.RBPR = nc.alloc_sbuf_tensor("rbpr", [P, 2 * T], BF16)
    E.rows_bf = nc.alloc_sbuf_tensor("rows_bf", [1, 2 * T], BF16)
    E.rowm = nc.alloc_sbuf_tensor("rowm", [1, T], BF16)
    PQR = [nc.alloc_sbuf_tensor(f"pq{i}", [P, T], BF16) for i in range(2)]
    KVR = [nc.alloc_sbuf_tensor(f"kvr{i}", [P, T], F32) for i in range(2)]
    KCR = [nc.alloc_sbuf_tensor(f"kcr{i}", [P, T], F32) for i in range(2)]
    TOT = nc.alloc_sbuf_tensor("TOT", [P, 32], F32)
    sgs = nc.alloc_sbuf_tensor("sgs", [P, 64], F32)
    sus = nc.alloc_sbuf_tensor("sus", [P, 64], F32)
    sds = nc.alloc_sbuf_tensor("sds", [P, 16], F32)
    n1 = nc.alloc_sbuf_tensor("n1", [P, T], F32)
    d1 = nc.alloc_sbuf_tensor("d1", [P, T], F32)
    S = nc.alloc_psum_tensor("S", [P, 4 * 512], F32)
    pb45 = [nc.alloc_psum_tensor(f"pb{i}", [P, 512], F32) for i in (4, 5)]
    E.pb = [PBView(S, i) for i in range(4)] + [PBView(t, None) for t in pb45]
    pb = E.pb
    E.ln_war = None
    E.pb_war = {4: None, 5: None}
    pe, dve, act = E.sem("pe"), E.sem("dve"), E.sem("act")
    ldx1, ldsc, sto = E.sem("ldx1"), E.sem("ldsc"), E.sem("sto")

    for dc in range(NDC):
        E.dma("gpsimd", X[:, dc * T:(dc + 1) * T], x2_hbm[dc * P:(dc + 1) * P, :],
              ldx1)
    E.dma("gpsimd", TOT[:, :], ptot_hbm, ldx1)
    E.dma("gpsimd", sgs[:, :], sg_pk, ldsc)
    E.dma("gpsimd", sus[:, :], su_pk, ldsc)
    E.dma("gpsimd", sds[:, :], sd_pk, ldsc)
    x_at = ldx1.now
    sc_at = ldsc.now

    pqring = Ring(E, "pqr", [t[:, :] for t in PQR])
    kvring = Ring(E, "kvr", [t[:, :] for t in KVR])
    kcring = Ring(E, "kcr", [t[:, :] for t in KCR])
    op_at = None
    for j in range(NDC):
        pqv, pq_at = pqring.load("gpsimd", phiq_hbm[j * P:(j + 1) * P, :])
        kvv, kv_at = kvring.load("gpsimd", kvc_hbm[j * P:(j + 1) * P, :])
        kcv, kc_at = kcring.load("gpsimd", kc_hbm[j * P:(j + 1) * P, :])
        E.ts("vector", n1[:, :], kvv, TOT[:, j:j + 1], None, ALU.add,
             waits=[kv_at, x_at], inc=dve)
        E.tt("vector", n1[:, :], n1[:, :], pqv, ALU.mult, waits=[pq_at], inc=dve)
        E.ts("vector", d1[:, :], kcv, TOT[:, 16 + j:17 + j], None, ALU.add,
             waits=[kc_at], inc=dve)
        E.tt("vector", d1[:, :], d1[:, :], pqv, ALU.mult, inc=dve)
        E.ts("vector", d1[:, :], d1[:, :], 1e-9, None, ALU.add, inc=dve)
        E.recip(d1[:, :], d1[:, :], inc=dve)
        E.tt("vector", LNF[:, j * T:(j + 1) * T], n1[:, :], d1[:, :], ALU.mult,
             inc=dve)
        op_at = dve.now
        pqring.set_free(pqv, op_at)
        kvring.set_free(kvv, op_at)
        kcring.set_free(kcv, op_at)
    wring = Ring(E, "wr", [w[:, :] for w in WR])
    RS128 = float(1.0 / np.sqrt(np.float64(128.0)))
    for oc in range(NDC):
        slab, slab_at = wslab_load(E, wring, woutT, oc * P, P)
        last_pe = None
        for th in range(2):
            pbx = 4 + th
            pbb = pb[pbx]
            for dc in range(NDC):
                w = [slab_at]
                if dc == 0:
                    w += [E.pb_war[pbx], op_at]
                E.mm(pbb.full(), slab[:, dc * P:(dc + 1) * P],
                     LNF[:, dc * T + th * 512:dc * T + (th + 1) * 512],
                     start=(dc == 0), stop=(dc == NDC - 1), waits=w,
                     inc=pe if dc == NDC - 1 else None)
            mm_at = pe.now
            last_pe = mm_at
            xsl = X[:, oc * T + th * 512:oc * T + (th + 1) * 512]
            E.stt("vector", xsl, pbb.full(), RS128, xsl, ALU.mult, ALU.add,
                  waits=[mm_at, x_at], inc=dve)
            E.pb_war[pbx] = dve.now
        wring.set_free(slab, last_pe)
    x3_at = dve.now

    def src(dc):
        def get():
            return X[:, dc * T:(dc + 1) * T], [x3_at], (lambda s, c: None)
        return get
    ln_at = emit_ln(E, [src(dc) for dc in range(NDC)], LNF[:, :])

    for blk in range(8):
        for half, (wap, ss, fn) in enumerate(((wg_ap, sgs, AF.Silu),
                                              (wu_ap, sus, AF.Identity))):
            for fc in range(8):
                ic = blk * 8 + fc
                slab, slab_at = wslab_load(E, wring, wap, ic * P, P)
                last_pe = None
                for th in range(2):
                    pbx = 4 + th
                    pbb = pb[pbx]
                    for dc in range(NDC):
                        w = [slab_at]
                        if dc == 0:
                            w += [E.pb_war[pbx], ln_at]
                        E.mm(pbb.full(), slab[:, dc * P:(dc + 1) * P],
                             LNF[:, dc * T + th * 512:dc * T + (th + 1) * 512],
                             start=(dc == 0), stop=(dc == NDC - 1), waits=w,
                             inc=pe if dc == NDC - 1 else None)
                    mm_at = pe.now
                    last_pe = mm_at
                    dst = GU[:, half * 8 * T + fc * T + th * 512:
                             half * 8 * T + fc * T + (th + 1) * 512]
                    E.act(dst, pbb.full(), fn, scale=ss[:, ic:ic + 1],
                          waits=[mm_at, sc_at], inc=act)
                    E.pb_war[pbx] = act.now
                wring.set_free(slab, last_pe)
        gu_at = act.now
        for fc in range(8):
            E.tt("vector", GU[:, fc * T:(fc + 1) * T], GU[:, fc * T:(fc + 1) * T],
                 GU[:, 8 * T + fc * T:8 * T + (fc + 1) * T], ALU.mult,
                 waits=[gu_at] if fc == 0 else (), inc=dve)
        fuse_at = dve.now
        for oc in range(NDC):
            slab, slab_at = wslab_load(E, wring, wd_ap[blk * 1024:(blk + 1) * 1024, :],
                                       oc * P, P, nchunk=8)
            last_pe = None
            for th in range(2):
                pbx = 4 + th
                pbb = pb[pbx]
                for fc in range(8):
                    w = [slab_at, fuse_at]
                    if fc == 0:
                        w += [E.pb_war[pbx]]
                    E.mm(pbb.full(), slab[:, fc * P:(fc + 1) * P],
                         GU[:, fc * T + th * 512:fc * T + (th + 1) * 512],
                         start=(fc == 0), stop=(fc == 7), waits=w,
                         inc=pe if fc == 7 else None)
                mm_at = pe.now
                last_pe = mm_at
                xsl = X[:, oc * T + th * 512:oc * T + (th + 1) * 512]
                E.stt("vector", xsl, pbb.full(), sds[:, oc:oc + 1], xsl,
                      ALU.mult, ALU.add, waits=[mm_at, sc_at], inc=dve)
                E.pb_war[pbx] = dve.now
            wring.set_free(slab, last_pe)
    x4_at = dve.now
    for dc in range(NDC):
        E.dma("sync", x4_out[dc * P:(dc + 1) * P, :], X[:, dc * T:(dc + 1) * T],
              sto, waits=[x4_at])
    E.op("sync", lambda e: e.nop(), waits=[sto.now])


from concourse.bass_utils import run_bass_kernel_spmd


def _build_l0a():
    nc = bass.Bass("TRN2", target_bir_lowering=False, debug=False,
                   detect_race_conditions=False)
    xo = nc.dram_tensor("xo", [D, T], F32, kind="ExternalInput").ap()
    xp = nc.dram_tensor("xp", [D, T], F32, kind="ExternalInput").ap()
    wqkvT = nc.dram_tensor("wqkvT", [D, 3 * D], F32, kind="ExternalInput").ap()
    woutT = nc.dram_tensor("woutT", [D, D], F32, kind="ExternalInput").ap()
    bqpk = nc.dram_tensor("bqpk", [P, 48], F32, kind="ExternalInput").ap()
    bvrow = nc.dram_tensor("bvrow", [1, D], F32, kind="ExternalInput").ap()
    boutpk = nc.dram_tensor("boutpk", [P, 16], F32, kind="ExternalInput").ap()
    x1 = nc.dram_tensor("x1", [D, T], F32, kind="ExternalOutput").ap()
    E = Emit(nc)
    emit_l0_attn(E, xo, xp, wqkvT, bqpk, bvrow, woutT, boutpk, x1)
    E.build()
    return nc


def _build_l0f():
    nc = bass.Bass("TRN2", target_bir_lowering=False, debug=False,
                   detect_race_conditions=False)
    x1 = nc.dram_tensor("x1", [D, T], F32, kind="ExternalInput").ap()
    rwT = nc.dram_tensor("rwT", [D, 8], F32, kind="ExternalInput").ap()
    w1 = nc.dram_tensor("w1", [8, D, 1024], F32, kind="ExternalInput")
    w2 = nc.dram_tensor("w2", [8, D, 1024], F32, kind="ExternalInput")
    w3 = nc.dram_tensor("w3", [8, 1024, D], F32, kind="ExternalInput")
    x2 = nc.dram_tensor("x2", [D, T], F32, kind="ExternalOutput").ap()
    E = Emit(nc)
    emit_l0_ffn(E, x1, rwT, w1.ap(), w2.ap(), w3.ap(), x2)
    E.build()
    return nc


def _build_l1a1():
    nc = bass.Bass("TRN2", target_bir_lowering=False, debug=False,
                   detect_race_conditions=False)
    x2 = nc.dram_tensor("x2", [D, T], F32, kind="ExternalInput").ap()
    wqkvT = nc.dram_tensor("wqkvT", [D, 3 * D], F32, kind="ExternalInput").ap()
    phiq = nc.dram_tensor("phiq", [D, T], BF16, kind="ExternalOutput").ap()
    kvc = nc.dram_tensor("kvc", [D, T], F32, kind="ExternalOutput").ap()
    kc = nc.dram_tensor("kc", [D, T], F32, kind="ExternalOutput").ap()
    tot = nc.dram_tensor("tot", [P, 32], F32, kind="ExternalOutput").ap()
    E = Emit(nc)
    emit_l1a_part1(E, x2, wqkvT, phiq, kvc, kc, tot)
    E.build()
    return nc


def _build_l1a2():
    nc = bass.Bass("TRN2", target_bir_lowering=False, debug=False,
                   detect_race_conditions=False)
    x2 = nc.dram_tensor("x2", [D, T], F32, kind="ExternalInput").ap()
    phiq = nc.dram_tensor("phiq", [D, T], BF16, kind="ExternalInput").ap()
    kvc = nc.dram_tensor("kvc", [D, T], F32, kind="ExternalInput").ap()
    kc = nc.dram_tensor("kc", [D, T], F32, kind="ExternalInput").ap()
    ptot = nc.dram_tensor("ptot", [P, 32], F32, kind="ExternalInput").ap()
    woutT = nc.dram_tensor("woutT", [D, D], F32, kind="ExternalInput").ap()
    wg = nc.dram_tensor("wg", [D, 8192], I8, kind="ExternalInput").ap()
    wu = nc.dram_tensor("wu", [D, 8192], I8, kind="ExternalInput").ap()
    wd = nc.dram_tensor("wd", [8192, D], I8, kind="ExternalInput").ap()
    sg = nc.dram_tensor("sg", [P, 64], F32, kind="ExternalInput").ap()
    su = nc.dram_tensor("su", [P, 64], F32, kind="ExternalInput").ap()
    sd = nc.dram_tensor("sd", [P, 16], F32, kind="ExternalInput").ap()
    x4 = nc.dram_tensor("x4", [D, T], F32, kind="ExternalOutput").ap()
    E = Emit(nc)
    emit_l1a2_l1f(E, x2, phiq, kvc, kc, ptot, woutT, wg, wu, wd, sg, su, sd, x4)
    E.build()
    return nc


def kernel(L0_attn_hidden_states, L0_attn_w_qkv, L0_attn_b_qkv, L0_attn_w_out,
           L0_attn_b_out, L0_ffn_router_w, L0_ffn_w1, L0_ffn_w2, L0_ffn_w3,
           L1_attn_W_qkv, L1_attn_W_out, L1_ffn_w_gate, L1_ffn_w_up,
           L1_ffn_w_down, L1_ffn_s_gate, L1_ffn_s_up, L1_ffn_s_down):
    x = np.ascontiguousarray(np.asarray(L0_attn_hidden_states, np.float32))
    wqkvT = np.ascontiguousarray(np.asarray(L0_attn_w_qkv, np.float32).T)
    bq = np.asarray(L0_attn_b_qkv, np.float32)
    bqpk = np.ascontiguousarray(bq.reshape(48, P).T)
    bvrow = np.ascontiguousarray(bq[2 * D:3 * D].reshape(1, D))
    woutT = np.ascontiguousarray(np.asarray(L0_attn_w_out, np.float32).T)
    boutpk = np.ascontiguousarray(
        np.asarray(L0_attn_b_out, np.float32).reshape(16, P).T)
    rwT = np.ascontiguousarray(np.asarray(L0_ffn_router_w, np.float32).T)
    w1 = np.ascontiguousarray(np.asarray(L0_ffn_w1, np.float32))
    w2 = np.ascontiguousarray(np.asarray(L0_ffn_w2, np.float32))
    w3 = np.ascontiguousarray(np.asarray(L0_ffn_w3, np.float32))
    l1qkvT = np.ascontiguousarray(np.asarray(L1_attn_W_qkv, np.float32).T)
    l1outT = np.ascontiguousarray(np.asarray(L1_attn_W_out, np.float32).T)
    wg = np.ascontiguousarray(np.asarray(L1_ffn_w_gate, np.int8))
    wu = np.ascontiguousarray(np.asarray(L1_ffn_w_up, np.int8))
    wd = np.ascontiguousarray(np.asarray(L1_ffn_w_down, np.int8))
    sg = np.ascontiguousarray(np.asarray(L1_ffn_s_gate, np.float32).reshape(64, P).T)
    su = np.ascontiguousarray(np.asarray(L1_ffn_s_up, np.float32).reshape(64, P).T)
    sd = np.ascontiguousarray(np.asarray(L1_ffn_s_down, np.float32).reshape(16, P).T)
    zt = np.zeros((P, 32), np.float32)

    xT = [np.ascontiguousarray(x[b].T) for b in range(4)]

    def core_bh(c):
        return c // 2, c % 2

    cores = list(range(8))
    in1 = []
    for c in cores:
        b, h = core_bh(c)
        in1.append({"xo": xT[b][:, h * T:(h + 1) * T],
                    "xp": xT[b][:, (1 - h) * T:(2 - h) * T],
                    "wqkvT": wqkvT, "woutT": woutT, "bqpk": bqpk,
                    "bvrow": bvrow, "boutpk": boutpk})
    r1 = run_bass_kernel_spmd(_build_l0a(), in1, core_ids=cores)
    x1s = [np.asarray(r1.results[c]["x1"]) for c in cores]
    in2 = [{"x1": x1s[c], "rwT": rwT, "w1": w1, "w2": w2, "w3": w3}
           for c in cores]
    r2 = run_bass_kernel_spmd(_build_l0f(), in2, core_ids=cores)
    x2s = [np.asarray(r2.results[c]["x2"]) for c in cores]
    in3 = [{"x2": x2s[c], "wqkvT": l1qkvT} for c in cores]
    r3 = run_bass_kernel_spmd(_build_l1a1(), in3, core_ids=cores)
    in4 = []
    for c in cores:
        b, h = core_bh(c)
        ptot = zt if h == 0 else np.asarray(r3.results[2 * b]["tot"], np.float32)
        in4.append({"x2": x2s[c],
                    "phiq": np.asarray(r3.results[c]["phiq"]),
                    "kvc": np.asarray(r3.results[c]["kvc"], np.float32),
                    "kc": np.asarray(r3.results[c]["kc"], np.float32),
                    "ptot": ptot, "woutT": l1outT,
                    "wg": wg, "wu": wu, "wd": wd,
                    "sg": sg, "su": su, "sd": sd})
    r4 = run_bass_kernel_spmd(_build_l1a2(), in4, core_ids=cores)
    out = np.empty((4, 2 * T, D), np.float32)
    for c in cores:
        b, h = core_bh(c)
        out[b, h * T:(h + 1) * T, :] = np.asarray(r4.results[c]["x4"],
                                                  np.float32).T
    return out


# version 02 (final)
# speedup vs baseline: 1.0259x; 1.0259x over previous
import numpy as np
import ml_dtypes
import concourse.bass as bass
import concourse.mybir as mybir

F32 = mybir.dt.float32
BF16 = mybir.dt.bfloat16
I8 = mybir.dt.int8
AF = mybir.ActivationFunctionType
ALU = mybir.AluOpType
AX = mybir.AxisListType
P = 128
T = 1024
D = 2048
NDC = D // P
NTC = T // P
EPS_LN = 1e-5
RSQRT_D = float(1.0 / np.sqrt(np.float64(2048.0)))
BF = ml_dtypes.bfloat16


class Sem:
    def __init__(self, nc, name):
        self.h = nc.alloc_semaphore(name)
        self.n = 0

    def inc(self, k):
        self.n += k
        return self.n

    @property
    def now(self):
        return (self, self.n)


class Emit:
    ENGINES = ("tensor", "vector", "scalar", "gpsimd", "sync")

    def __init__(self, nc):
        self.nc = nc
        self.ops = {e: [] for e in self.ENGINES}
        self._sems = {}

    def sem(self, name):
        if name not in self._sems:
            self._sems[name] = Sem(self.nc, name)
        return self._sems[name]

    SERIALIZE = ("vector", "scalar")

    def op(self, eng, fn, waits=(), inc=None, inc_n=1):
        waits = [w for w in waits if w is not None]
        if eng in self.SERIALIZE:
            last = getattr(self, "_last_op", {}).get(eng)
            if last is not None:
                waits = waits + [last]
            if inc is None:
                inc = self.sem(f"auto_{eng}")
                inc_n = 1
        if inc is not None:
            inc.inc(inc_n)
        if eng in self.SERIALIZE:
            if not hasattr(self, "_last_op"):
                self._last_op = {}
            self._last_op[eng] = (inc, inc.n)
        self.ops[eng].append({"fn": fn, "waits": waits, "inc": inc, "inc_n": inc_n})

    def dma(self, q, out, in_, sem, waits=()):
        self.op(q, lambda e, out=out, in_=in_: e.dma_start(out=out, in_=in_),
                waits, sem, 16)

    def mm(self, out, lhsT, rhs, start, stop, waits=(), inc=None):
        self.op("tensor",
                lambda e, o=out, l=lhsT, r=rhs, s=start, p=stop:
                e.matmul(o, l, r, start=s, stop=p), waits, inc)

    def trans(self, out, in_, waits=(), inc=None, ident=None):
        self.op("tensor",
                lambda e, o=out, i=in_, idn=ident:
                e.transpose(o, i, idn if idn is not None else self.ident[:, :]),
                waits, inc)

    def copy(self, eng, out, in_, waits=(), inc=None):
        self.op(eng, lambda e, o=out, i=in_: e.tensor_copy(o, i), waits, inc)

    def tt(self, eng, out, a, b, alu, waits=(), inc=None):
        self.op(eng, lambda e, o=out, a=a, b=b, u=alu: e.tensor_tensor(o, a, b, u),
                waits, inc)

    def ts(self, eng, out, in_, s1, s2, op0, op1=None, accum=None, waits=(),
           inc=None):
        def fn(e, o=out, i=in_, s1=s1, s2=s2, op0=op0, op1=op1, a=accum):
            if op1 is None:
                return e.tensor_scalar(o, i, s1, None, op0, accum_out=a)
            return e.tensor_scalar(o, i, s1, s2, op0, op1, accum_out=a)
        self.op(eng, fn, waits, inc)

    def stt(self, eng, out, in0, scalar, in1, op0, op1, waits=(), inc=None):
        self.op(eng,
                lambda e, o=out, i0=in0, s=scalar, i1=in1, a=op0, b=op1:
                e.scalar_tensor_tensor(o, i0, s, i1, a, b), waits, inc)

    def act(self, out, in_, func, bias=0.0, scale=1.0, accum=None, waits=(),
            inc=None):
        self.op("scalar",
                lambda e, o=out, i=in_, f=func, b=bias, s=scale, a=accum:
                e.activation(o, i, f, bias=b, scale=s, accum_out=a), waits, inc)

    def red(self, eng, out, in_, alu, negate=False, waits=(), inc=None):
        self.op(eng,
                lambda e, o=out, i=in_, u=alu, n=negate:
                e.tensor_reduce(o, i, AX.X, u, negate=n), waits, inc)

    def recip(self, out, in_, waits=(), inc=None):
        self.op("vector", lambda e, o=out, i=in_: e.reciprocal(o, i), waits, inc)

    def scan_cs(self, out, data, initial, waits=(), inc=None):
        self.op("vector",
                lambda e, o=out, d=data, i=initial:
                e.tensor_tensor_scan(o, d, d, i, ALU.add, ALU.bypass), waits, inc)

    def dbg(self, name, view, waits=()):
        if not getattr(self, "debug_dumps", False):
            return
        shp = [view.shape[0], int(np.prod(view.shape[1:]))]
        t = self.nc.dram_tensor(f"dbg_{name}", list(view.shape), view.dtype,
                                kind="ExternalOutput")
        self.dma("sync", t.ap(), view, self.sem("sto_dbg"), waits=waits)

    def build(self):
        with self.nc.Block() as block:
            for name in self.ENGINES:
                ops = self.ops[name]

                def body(eng, ops=ops):
                    for o in ops:
                        for (s, v) in o["waits"]:
                            eng.wait_ge(s.h, v)
                        inst = o["fn"](eng)
                        if o["inc"] is not None:
                            inst.then_inc(o["inc"].h, o["inc_n"])
                getattr(block, name)(body)


class Ring:

    def __init__(self, E, name, views):
        self.E = E
        self.views = list(views)
        self.n = len(views)
        self.sems = [E.sem(f"{name}_s{k}") for k in range(self.n)]
        self.freed = [None] * self.n
        self.i = 0

    def load(self, q, in_ap, waits=(), dst=None):
        k = self.i % self.n
        self.i += 1
        view = self.views[k]
        war = self.freed[k]
        sem = self.sems[k]
        self.E.dma(q, view if dst is None else dst(view), in_ap, sem,
                   waits=([war] if war else []) + list(waits))
        return view, (sem, sem.n)

    def set_free(self, view, sem_cnt):
        for k, v in enumerate(self.views):
            if v is view:
                self.freed[k] = sem_cnt
                return
        raise KeyError("view not in ring")


def alloc_consts(E):
    nc = E.nc
    E.ident_d = nc.inline_tensor(np.eye(P, dtype=BF), name="c_ident")
    E.onesr_d = nc.inline_tensor(np.ones((1, P), dtype=BF), name="c_onesr")
    E.onesc_d = nc.inline_tensor(np.ones((P, 1), dtype=BF), name="c_onesc")
    E.ident = nc.alloc_sbuf_tensor("ident", [P, P], BF16)
    E.ones_row = nc.alloc_sbuf_tensor("ones_row", [1, P], BF16)
    E.ones_col = nc.alloc_sbuf_tensor("ones_col", [P, 1], BF16)
    ld = E.sem("ldc")
    E.dma("gpsimd", E.ident[:, :], E.ident_d.ap(), ld)
    E.dma("gpsimd", E.ones_row[:, :], E.onesr_d.ap(), ld)
    E.dma("gpsimd", E.ones_col[:, :], E.onesc_d.ap(), ld)
    E.identf_d = nc.inline_tensor(np.eye(P, dtype=np.float32), name="c_identf")
    E.ident_f32 = nc.alloc_sbuf_tensor("ident_f32", [P, P], F32)
    E.dma("gpsimd", E.ident_f32[:, :], E.identf_d.ap(), ld)
    cv = np.array([[EPS_LN, 1e-9, 0.0, 0.0]], dtype=np.float32)
    E.cvals_d = nc.inline_tensor(cv, name="c_vals")
    E.cvals = nc.alloc_sbuf_tensor("cvals", [1, 4], F32)
    E.dma("gpsimd", E.cvals[:, :], E.cvals_d.ap(), ld)
    E.consts_at = ld.now


def emit_ln(E, src_tiles, out):
    pe, dve, act = E.sem("pe"), E.sem("dve"), E.sem("act")
    pb = E.pb
    sq = E.SCR[:, 0:T]
    scr = E.SCR[:, T:2 * T]
    sq_pe_at = None
    for dc in range(NDC):
        src, waits, free_sink = src_tiles[dc]()
        ocol = out[:, dc * T:(dc + 1) * T]
        w = list(waits)
        if dc == 0:
            w += [E.ln_war]
        E.copy("vector", ocol, src, waits=w, inc=dve)
        cast_at = dve.now
        free_sink(*cast_at)
        E.act(sq, ocol, AF.Square,
              waits=[cast_at] + ([sq_pe_at] if sq_pe_at else []), inc=act)
        sq_at = act.now
        for th in range(2):
            E.mm(pb[th][0:1, :], E.ones_col[:, :],
                 ocol[:, th * 512:(th + 1) * 512],
                 start=(dc == 0), stop=(dc == NDC - 1), waits=[cast_at, E.consts_at],
                 inc=pe if dc == NDC - 1 else None)
            E.mm(pb[2 + th][0:1, :], E.ones_col[:, :], sq[:, th * 512:(th + 1) * 512],
                 start=(dc == 0), stop=(dc == NDC - 1), waits=[sq_at], inc=pe)
        sq_pe_at = pe.now
    sums_at = pe.now
    dump_gate = None
    if getattr(E, "debug_dumps", False):
        sd = E.sem("sto_dbg")
        E.dbg(f"xpre{E._ln_ctr}", out[:, :], waits=[sums_at])
        dump_gate = (sd, sd.n)
    rbf = E.rows_bf
    for th in range(2):
        q = pb[2 + th][0:1, :]
        tmp = pb[4 + th][0:1, :]
        m_sb = E.rowm[0:1, 0:512]
        msq_sb = E.rowm[0:1, 512:1024]
        E.ts("vector", m_sb, pb[th][0:1, :], 1.0 / D, None, ALU.mult,
             waits=[sums_at, E.pb_war[4 + th]], inc=dve)
        E.ts("vector", q, q, 1.0 / D, None, ALU.mult, inc=dve)
        E.tt("vector", msq_sb, m_sb, m_sb, ALU.mult, inc=dve)
        E.tt("vector", q, q, msq_sb, ALU.subtract, inc=dve)
        E.act(tmp, q, AF.Sqrt, bias=E.cvals[0:1, 0:1],
              waits=[dve.now, E.consts_at], inc=act)
        E.recip(q, tmp, waits=[act.now], inc=dve)
        E.copy("vector", rbf[0:1, th * 512:(th + 1) * 512], q, inc=dve)
        E.tt("vector", rbf[0:1, T + th * 512:T + (th + 1) * 512], m_sb, q,
             ALU.mult, inc=dve)
    rows_at = dve.now
    for j, base in enumerate((0, T)):
        for th in range(2):
            pbb = pb[4 + th]
            E.mm(pbb[:, :], E.ones_row[:, :],
                 rbf[0:1, base + th * 512: base + (th + 1) * 512],
                 start=True, stop=True,
                 waits=[rows_at, E.consts_at, E.pb_war[4 + th]], inc=pe)
            E.copy("vector", E.RBPR[:, base + th * 512: base + (th + 1) * 512],
                   pbb[:, :], waits=[pe.now], inc=dve)
            E.pb_war[4 + th] = dve.now
    bcast_at = dve.now
    if getattr(E, "debug_dumps", False):
        E.dbg(f"rbf{E._ln_ctr}", E.rows_bf[0:1, :], waits=[rows_at])
        E.dbg(f"rbpr{E._ln_ctr}", E.RBPR[:, :], waits=[bcast_at])
        E._ln_ctr += 1
    for dc in range(NDC):
        ocol = out[:, dc * T:(dc + 1) * T]
        E.tt("vector", scr, ocol, E.RBPR[:, 0:T], ALU.mult,
             waits=([bcast_at] + ([dump_gate] if dc == 0 and dump_gate else []))
             if dc == 0 else (), inc=dve)
        E.tt("vector", ocol, scr, E.RBPR[:, T:2 * T], ALU.subtract, inc=dve)
    done = dve.now
    E.ln_war = done
    return done


def hbm_tile_src(E, hbm_ap, ring, q="gpsimd"):
    def make(dc):
        def get():
            view, at = ring.load(q, hbm_ap[dc * P:(dc + 1) * P, :])

            def free_sink(sem, cnt, view=view):
                ring.set_free(view, (sem, cnt))
            return view, [at], free_sink
        return get
    return [make(dc) for dc in range(NDC)]


def wslab_load(E, ring, wT_ap, col0, ncols, nchunk=NDC):
    srcap = wT_ap[:, col0:col0 + ncols].rearrange("(dc p) j -> p dc j", p=P)
    view, at = ring.load("gpsimd", srcap,
                         dst=lambda v: v[:, 0:nchunk * ncols].rearrange(
                             "p (dc j) -> p dc j", j=ncols))
    return view, at


def emit_l0_attn(E, own_xT, par_xT, wqkvT, bqkv_pk, bqkv_vrow, woutT, bout_pk,
                 x1_out=None, x1_res=None):
    nc = E.nc
    sh = getattr(E, "shared", None)
    if sh is None:
        alloc_consts(E)
        A = nc.alloc_sbuf_tensor("regA", [P, NDC * T], BF16)
        Bf = nc.alloc_sbuf_tensor("regB", [P, NDC * T], BF16)
        C = nc.alloc_sbuf_tensor("regC", [P, NDC * 2 * T], BF16)
        Dq = nc.alloc_sbuf_tensor("regD", [P, NDC * T], BF16)
        WR = [nc.alloc_sbuf_tensor(f"wr{i}", [P, NDC * 256], BF16) for i in range(2)]
        XR = [nc.alloc_sbuf_tensor(f"xr{i}", [P, T], F32) for i in range(2)]
        E.SCR = nc.alloc_sbuf_tensor("scr2", [P, 2 * T], BF16)
        E.RBPR = nc.alloc_sbuf_tensor("rbpr", [P, 2 * T], BF16)
        E.rows_bf = nc.alloc_sbuf_tensor("rows_bf", [1, 2 * T], BF16)
        E.rowm = nc.alloc_sbuf_tensor("rowm", [1, T], BF16)
        S = nc.alloc_psum_tensor("S", [P, 4 * 512], F32)
        pb45 = [nc.alloc_psum_tensor(f"pb{i}", [P, 512], F32) for i in (4, 5)]
        E.pb = [PBView(S, i) for i in range(4)] + [PBView(t, None) for t in pb45]
        E.ptb = [nc.alloc_psum_tensor(f"ptb{i}", [P, P], BF16) for i in range(2)]
        E.Sps = S
        E.ln_war = None
        E.pb_war = {4: None, 5: None}
    else:
        A, Bf, C, Dq, WR, XR = sh["A"], sh["B"], sh["C"], sh["D"], sh["WR"], sh["XR"]
    S = E.Sps
    ptb = E.ptb
    pb = E.pb
    if x1_out is not None:
        X1R = [nc.alloc_sbuf_tensor(f"x1r{i}", [P, 512], F32) for i in range(2)]
    bvb = nc.alloc_sbuf_tensor("bvb", [P, D], BF16)
    bq_sb = nc.alloc_sbuf_tensor("bq_sb", [P, 48], F32)
    bout_sb = nc.alloc_sbuf_tensor("bout_sb", [P, 16], F32)
    smax4 = nc.alloc_sbuf_tensor("smax4", [P, 4], F32)
    smax = nc.alloc_sbuf_tensor("smax", [P, 1], F32)
    sbias = nc.alloc_sbuf_tensor("sbias", [P, 1], F32)
    sden = nc.alloc_sbuf_tensor("sden", [P, 1], F32)
    srec = nc.alloc_sbuf_tensor("srec", [P, 1], F32)

    pe, dve, act = E.sem("pe"), E.sem("dve"), E.sem("act")
    ldb = E.sem("ldb")
    sto = [E.sem("sto0"), E.sem("sto1")] if x1_out is not None else None

    E.dma("gpsimd", bq_sb[:, :], bqkv_pk, ldb)
    E.dma("gpsimd", bout_sb[:, :], bout_pk, ldb)
    bias_at = ldb.now

    xring = Ring(E, "xr", [x[:, :] for x in XR])
    lnO_at = emit_ln(E, hbm_tile_src(E, own_xT, xring), A[:, :])
    E.dbg("lnO", A[:, :], waits=[lnO_at])
    lnP_at = emit_ln(E, hbm_tile_src(E, par_xT, xring), Bf[:, :])

    ldbv = E.sem("ldbv")
    bvrow = E.rows_bf
    E.dma("gpsimd", bvrow[0:1, 0:D], bqkv_vrow, ldbv, waits=[lnP_at])
    bvrow_at = ldbv.now
    for kb in range(4):
        pbb = pb[4 + kb % 2]
        E.mm(pbb.full(), E.ones_row[:, :], bvrow[0:1, kb * 512:(kb + 1) * 512],
             start=True, stop=True,
             waits=[bvrow_at, E.consts_at, E.pb_war[4 + kb % 2]], inc=pe)
        E.copy("vector", bvb[:, kb * 512:(kb + 1) * 512], pbb.full(),
               waits=[pe.now], inc=dve)
        E.pb_war[4 + kb % 2] = dve.now
    bvb_at = dve.now

    wring = getattr(E, "wring", None) or Ring(E, "wr", [w[:, :] for w in WR])

    def proj_fm(out_cb, wT, wcol0, nwoc, spans, bias_col, wslab_cols=P):
        done = None
        for oc in range(nwoc):
            slab, slab_at = wslab_load(E, wring, wT, wcol0 + oc * wslab_cols,
                                       wslab_cols)
            tb_global = 0
            last_pe_for_slab = None
            for (region_at, region, col_fn) in spans:
                for tb in range(2):
                    pbx = 4 + tb_global % 2
                    pbb = pb[pbx]
                    for dc in range(NDC):
                        w = [slab_at]
                        if dc == 0:
                            w += [E.pb_war[pbx], region_at]
                        E.mm(pbb.full(),
                             slab[:, dc * wslab_cols:dc * wslab_cols + P],
                             col_fn(dc, tb),
                             start=(dc == 0), stop=(dc == NDC - 1), waits=w,
                             inc=pe if dc == NDC - 1 else None)
                    mm_at = pe.now
                    last_pe_for_slab = mm_at
                    E.act(out_cb(oc, tb_global), pbb.full(), AF.Identity,
                          bias=bias_col(oc), waits=[mm_at, bias_at], inc=act)
                    E.pb_war[pbx] = act.now
                    tb_global += 1
            wring.set_free(slab, last_pe_for_slab)
            done = act.now
        return done

    q_at = proj_fm(
        lambda oc, tb: Dq[:, oc * T + tb * 512:oc * T + (tb + 1) * 512],
        wqkvT, 0, NDC,
        [(lnO_at, A, lambda dc, tb: A[:, dc * T + tb * 512:dc * T + (tb + 1) * 512])],
        lambda oc: bq_sb[:, oc:oc + 1])

    E.dbg("q", Dq[:, :], waits=[q_at])
    k_at = proj_fm(
        lambda oc, tb: C[:, oc * 2 * T + tb * 512:oc * 2 * T + (tb + 1) * 512],
        wqkvT, D, NDC,
        [(lnO_at, A, lambda dc, tb: A[:, dc * T + tb * 512:dc * T + (tb + 1) * 512]),
         (lnP_at, Bf, lambda dc, tb: Bf[:, dc * T + tb * 512:dc * T + (tb + 1) * 512])],
        lambda oc: bq_sb[:, 16 + oc:17 + oc])

    E.dbg("k", C[:, :], waits=[k_at])
    probs = E.RBPR
    prev_exp_at = None
    ptb_war = {0: None, 1: None}
    pt_done = []
    for qc in range(NTC):
        for kb in range(4):
            for dc in range(NDC):
                w = []
                if dc == 0:
                    w = [q_at, k_at]
                    if prev_exp_at is not None:
                        w.append(prev_exp_at)
                E.mm(S[:, kb * 512:(kb + 1) * 512],
                     Dq[:, dc * T + qc * P:dc * T + (qc + 1) * P],
                     C[:, dc * 2 * T + kb * 512:dc * 2 * T + (kb + 1) * 512],
                     start=(dc == 0), stop=(dc == NDC - 1), waits=w,
                     inc=pe if dc == NDC - 1 else None)
        s_at = pe.now
        for kb in range(4):
            E.red("vector", smax4[:, kb:kb + 1], S[:, kb * 512:(kb + 1) * 512],
                  ALU.max, waits=[s_at] if kb == 0 else (), inc=dve)
        E.red("vector", smax[:, 0:1], smax4[:, 0:4], ALU.max, inc=dve)
        E.ts("vector", sbias[:, 0:1], smax[:, 0:1], -RSQRT_D, None, ALU.mult,
             inc=dve)
        bias_ready = dve.now
        E.act(probs[:, 0:2 * T], S[:, :], AF.Exp, bias=sbias[:, 0:1],
              scale=RSQRT_D,
              waits=[bias_ready, s_at] + ([lnP_at] if qc == 0 else []) +
              ([ptb_war[0], ptb_war[1]] if qc else []), inc=act)
        exp_at = act.now
        prev_exp_at = exp_at
        E.red("vector", sden[:, 0:1], probs[:, 0:2 * T], ALU.add,
              waits=[exp_at], inc=dve)
        E.recip(srec[:, 0:1], sden[:, 0:1], waits=[dve.now], inc=dve)
        rec_at = dve.now
        E.ts("vector", probs[:, 0:2 * T], probs[:, 0:2 * T], srec[:, 0:1], None,
             ALU.mult, waits=[rec_at], inc=dve)
        norm_at = dve.now
        for kc in range(2 * NTC):
            pt = ptb[kc % 2]
            E.trans(pt[:, :], probs[:, kc * P:(kc + 1) * P],
                    waits=[norm_at, E.consts_at] +
                    ([ptb_war[kc % 2]] if ptb_war[kc % 2] else []), inc=pe)
            E.copy("vector", Dq[:, kc * T + qc * P: kc * T + (qc + 1) * P],
                   pt[:, :], waits=[pe.now], inc=dve)
            ptb_war[kc % 2] = dve.now
        if qc == 0:
            E.dbg("probs0", probs[:, 0:2 * T], waits=[norm_at])
        pt_done.append(dve.now)
    pt_all = pt_done[-1]
    E.dbg("pt", Dq[:, :], waits=[pt_all])

    v_done = None
    for hb in range(8):
        slab, slab_at = wslab_load(E, wring, wqkvT, 2 * D + hb * 256, 256)
        last_pe = None
        for tc in range(2 * NTC):
            reg, reg_at = (A, lnO_at) if tc < NTC else (Bf, lnP_at)
            tloc = tc % NTC
            pbx = 4 + tc % 2
            pbb = pb[pbx]
            for dc in range(NDC):
                w = [slab_at]
                if dc == 0:
                    w += [E.pb_war[pbx], reg_at]
                E.mm(pbb.part(0, 256),
                     reg[:, dc * T + tloc * P:dc * T + (tloc + 1) * P],
                     slab[:, dc * 256:(dc + 1) * 256],
                     start=(dc == 0), stop=(dc == NDC - 1), waits=w,
                     inc=pe if dc == NDC - 1 else None)
            mm_at = pe.now
            last_pe = mm_at
            E.tt("vector", C[:, tc * D + hb * 256:tc * D + (hb + 1) * 256],
                 pbb.part(0, 256), bvb[:, hb * 256:(hb + 1) * 256], ALU.add,
                 waits=[mm_at, bvb_at, pt_all], inc=dve)
            E.pb_war[pbx] = dve.now
        wring.set_free(slab, last_pe)
        v_done = dve.now

    E.dbg("v", C[:, :], waits=[v_done])
    ctx_done = None
    for hvc in range(NDC):
        for qh in range(2):
            pbx = 4 + (hvc * 2 + qh) % 2
            pbb = pb[pbx]
            for kc in range(2 * NTC):
                w = [v_done, pt_all]
                if kc == 0:
                    w += [E.pb_war[pbx]]
                E.mm(pbb.full(),
                     C[:, kc * D + hvc * P:kc * D + (hvc + 1) * P],
                     Dq[:, kc * T + qh * 512:kc * T + (qh + 1) * 512],
                     start=(kc == 0), stop=(kc == 2 * NTC - 1),
                     waits=w if kc == 0 else (),
                     inc=pe if kc == 2 * NTC - 1 else None)
            mm_at = pe.now
            E.act(A[:, hvc * T + qh * 512:hvc * T + (qh + 1) * 512], pbb.full(),
                  AF.Copy, waits=[mm_at], inc=act)
            E.pb_war[pbx] = act.now
            ctx_done = act.now

    E.dbg("ctx", A[:, :], waits=[ctx_done])
    if x1_out is not None:
        x1r_views = [x[:, :] for x in X1R]
        x1r_freed = [None, None]
        x1r_i = [0]
    for oc in range(NDC):
        slab, slab_at = wslab_load(E, wring, woutT, oc * P, P)
        xv, x0_at = xring.load("gpsimd", own_xT[oc * P:(oc + 1) * P, :])
        last_pe = None
        for th in range(2):
            pbx = 4 + th % 2
            pbb = pb[pbx]
            for hvc in range(NDC):
                w = [slab_at, ctx_done]
                if hvc == 0:
                    w += [E.pb_war[pbx]]
                E.mm(pbb.full(), slab[:, hvc * P:(hvc + 1) * P],
                     A[:, hvc * T + th * 512:hvc * T + (th + 1) * 512],
                     start=(hvc == 0), stop=(hvc == NDC - 1),
                     waits=w if hvc == 0 else [slab_at],
                     inc=pe if hvc == NDC - 1 else None)
            mm_at = pe.now
            last_pe = mm_at
            if x1_out is not None:
                k = x1r_i[0] % 2
                x1r_i[0] += 1
                ov = x1r_views[k]
                owar = x1r_freed[k]
                E.stt("vector", ov, pbb.full(), bout_sb[:, oc:oc + 1],
                      xv[:, th * 512:(th + 1) * 512], ALU.add, ALU.add,
                      waits=[mm_at, x0_at, bias_at] + ([owar] if owar else []),
                      inc=dve)
                E.pb_war[pbx] = dve.now
                E.dma("sync", x1_out[oc * P:(oc + 1) * P, th * 512:(th + 1) * 512],
                      ov, sto[k], waits=[dve.now])
                x1r_freed[k] = (sto[k], sto[k].n)
            else:
                ov = x1_res[:, oc * T + th * 512:oc * T + (th + 1) * 512]
                E.stt("vector", ov, pbb.full(), bout_sb[:, oc:oc + 1],
                      xv[:, th * 512:(th + 1) * 512], ALU.add, ALU.add,
                      waits=[mm_at, x0_at, bias_at], inc=dve)
                E.pb_war[pbx] = dve.now
        xring.set_free(xv, dve.now)
        wring.set_free(slab, last_pe)
    if x1_out is not None:
        E.op("sync", lambda e: e.nop(), waits=[sto[0].now, sto[1].now])
    return dve.now


class PBView:

    def __init__(self, tensor, slice_idx):
        self.t = tensor
        self.i = slice_idx

    def __getitem__(self, key):
        if self.i is None:
            return self.t[key]
        rows, cols = key
        c0, c1 = cols.start or 0, cols.stop if cols.stop is not None else 512
        return self.t[rows, self.i * 512 + c0:self.i * 512 + c1]

    def full(self):
        return self[0:P, 0:512]

    def part(self, c0, c1):
        return self[0:P, c0:c1]


def emit_l0_ffn(E, x1_hbm, rwT, w1_ap, w2_ap, w3_ap, x2_out,
                x1_res=None, x1_res_at=None):
    nc = E.nc
    sh = getattr(E, "shared", None)
    if sh is None:
        alloc_consts(E)
        X1 = nc.alloc_sbuf_tensor("X1", [P, NDC * T], F32)
        LNF = nc.alloc_sbuf_tensor("LNF", [P, NDC * T], BF16)
        GB = nc.alloc_sbuf_tensor("GB", [P, 8 * T], BF16)
        VB = nc.alloc_sbuf_tensor("VB", [P, 8 * T], BF16)
        WR = [nc.alloc_sbuf_tensor(f"wr{i}", [P, 16 * P], BF16) for i in range(2)]
        E.SCR = nc.alloc_sbuf_tensor("scr2", [P, 2 * T], BF16)
        E.RBPR = nc.alloc_sbuf_tensor("rbpr", [P, 2 * T], BF16)
        E.rows_bf = nc.alloc_sbuf_tensor("rows_bf", [1, 2 * T], BF16)
        E.rowm = nc.alloc_sbuf_tensor("rowm", [1, T], BF16)
        rw32 = nc.alloc_sbuf_tensor("rw32", [P, 16 * 8], F32)
        rwhl = nc.alloc_sbuf_tensor("rwhl", [P, 2 * 16 * 8], BF16)
        rwsn = nc.alloc_sbuf_tensor("rwsn", [8, 1], F32)
        LG = nc.alloc_sbuf_tensor("LG", [8, T], F32)
        Ltok = nc.alloc_sbuf_tensor("Ltok", [P, 8 * 8], F32)
        wtok = nc.alloc_sbuf_tensor("wtok", [P, 8 * 8], F32)
        msk = nc.alloc_sbuf_tensor("msk", [P, 8], F32)
        m1r = nc.alloc_sbuf_tensor("m1r", [P, 1], F32)
        m2r = nc.alloc_sbuf_tensor("m2r", [P, 1], F32)
        p1r = nc.alloc_sbuf_tensor("p1r", [P, 1], F32)
        p2r = nc.alloc_sbuf_tensor("p2r", [P, 1], F32)
        eq1 = nc.alloc_sbuf_tensor("eq1", [P, 8], F32)
        wfm8 = nc.alloc_sbuf_tensor("wfm8", [8, T], F32)
        wfm8b = nc.alloc_sbuf_tensor("wfm8b", [8, T], BF16)
        wrows = nc.alloc_sbuf_tensor("wrows", [1, 8 * T], BF16)
        WB = nc.alloc_sbuf_tensor("WB", [P, T], BF16)
        S = nc.alloc_psum_tensor("S", [P, 4 * 512], F32)
        pb45 = [nc.alloc_psum_tensor(f"pb{i}", [P, 512], F32) for i in (4, 5)]
        E.pb = [PBView(S, i) for i in range(4)] + [PBView(t, None) for t in pb45]
        E.Sps = S
        E.ln_war = None
        E.pb_war = {4: None, 5: None}
    else:
        X1 = x1_res
        LNF = sh["A"][:, :]
        GB = sh["D"][:, 0:8 * T]
        VB = sh["D"][:, 8 * T:16 * T]
        WR = sh["WR"]
        Bb = sh["B"]
        Bf32 = Bb[:, :].bitcast(F32)
        wrows = Bb[0:1, 0:8 * T]
        rwhl = Bb[:, 8192:8448]
        wfm8b = Bb[0:8, 8448:9472]
        WB = Bb[:, 9472:10496]
        rw32 = Bf32[:, 5248:5376]
        LG = Bf32[0:8, 5376:6400]
        wfm8 = Bf32[0:8, 6400:7424]
        Ltok = Bf32[:, 7424:7488]
        wtok = Bf32[:, 7488:7552]
        msk = Bf32[:, 7552:7560]
        eq1 = Bf32[:, 7560:7568]
        m1r = Bf32[:, 7568:7569]
        m2r = Bf32[:, 7569:7570]
        p1r = Bf32[:, 7570:7571]
        p2r = Bf32[:, 7571:7572]
        rwsn = Bf32[0:8, 7572:7573]
    S = E.Sps
    pb = E.pb

    pe, dve, act = E.sem("pe"), E.sem("dve"), E.sem("act")
    ldx1, sto = E.sem("ldx1"), E.sem("sto")

    if sh is None:
        for dc in range(NDC):
            E.dma("gpsimd", X1[:, dc * T:(dc + 1) * T],
                  x1_hbm[dc * P:(dc + 1) * P, :], ldx1)
        x1_at = ldx1.now
    else:
        x1_at = x1_res_at

    def src(dc):
        def get():
            return X1[:, dc * T:(dc + 1) * T], [x1_at], (lambda s, c: None)
        return get
    ln_at = emit_ln(E, [src(dc) for dc in range(NDC)], LNF[:, :])

    ldrw = E.sem("ldrw")
    for dc in range(NDC):
        E.dma("gpsimd", rw32[:, dc * 8:(dc + 1) * 8],
              rwT[dc * P:(dc + 1) * P, :], ldrw)
    rw_at = ldrw.now
    E.copy("vector", rwhl[:, 0:128], rw32[:, :], waits=[rw_at], inc=dve)
    E.tt("vector", rwhl[:, 128:256], rw32[:, :], rwhl[:, 0:128], ALU.subtract,
         waits=[dve.now], inc=dve)
    rwhl_at = dve.now
    ps8 = S[0:8, 1024:1025]
    for j, col0 in enumerate((0, 128)):
        for dc in range(NDC):
            E.mm(ps8, rwhl[:, col0 + dc * 8:col0 + (dc + 1) * 8],
                 E.ones_col[:, :], start=(j == 0 and dc == 0),
                 stop=(j == 1 and dc == NDC - 1),
                 waits=[rwhl_at, ln_at] if j == 0 and dc == 0 else (),
                 inc=pe if (j == 1 and dc == NDC - 1) else None)
    E.ts("vector", rwsn[:, :], ps8, -1.0, None, ALU.mult, waits=[pe.now],
         inc=dve)
    rwsn_at = dve.now
    PL = S[0:8, 0:1024]
    hi = E.SCR[:, 0:T]
    lo = E.SCR[:, T:2 * T]
    prev_mm = None
    for dc in range(NDC):
        E.copy("vector", hi, X1[:, dc * T:(dc + 1) * T],
               waits=[prev_mm, ln_at], inc=dve)
        E.tt("vector", lo, X1[:, dc * T:(dc + 1) * T], hi, ALU.subtract,
             inc=dve)
        cast_at = dve.now
        for si, (wcol, xbuf) in enumerate(((0, hi), (0, lo), (128, hi))):
            for th in range(2):
                E.mm(PL[0:8, th * 512:(th + 1) * 512],
                     rwhl[:, wcol + dc * 8:wcol + (dc + 1) * 8],
                     xbuf[:, th * 512:(th + 1) * 512],
                     start=(dc == 0 and si == 0),
                     stop=(dc == NDC - 1 and si == 2),
                     waits=[cast_at] if si == 0 and th == 0 else (),
                     inc=pe if (si == 2 and th == 1) else None)
        prev_mm = pe.now
    raw_at = pe.now
    E.tt("vector", LG[0:8, :], PL, E.RBPR[0:8, 0:T], ALU.mult,
         waits=[raw_at], inc=dve)
    E.stt("vector", LG[0:8, :], E.RBPR[0:8, T:2 * T], rwsn[:, :], LG[0:8, :],
          ALU.mult, ALU.add, waits=[dve.now, rwsn_at], inc=dve)
    lg_at = dve.now
    tr_war = {4: E.pb_war[4], 5: E.pb_war[5]}
    for tc in range(NTC):
        pbx = 4 + tc % 2
        E.trans(pb[pbx].part(0, 8), LG[0:8, tc * P:(tc + 1) * P],
                waits=[lg_at, E.consts_at, tr_war[pbx]], inc=pe,
                ident=E.ident_f32[0:8, 0:8])
        E.copy("vector", Ltok[:, tc * 8:(tc + 1) * 8], pb[pbx].part(0, 8),
               waits=[pe.now], inc=dve)
        tr_war[pbx] = dve.now
    dbgrow = nc.alloc_sbuf_tensor("dbgrow", [P, NTC * 4], F32)
    dbge = nc.alloc_sbuf_tensor("dbge", [P, NTC * 16], F32)
    for tc in range(NTC):
        lt = Ltok[:, tc * 8:(tc + 1) * 8]
        wt = wtok[:, tc * 8:(tc + 1) * 8]
        E.red("vector", m1r[:, :], lt, ALU.max, waits=[tr_war[4], tr_war[5]]
              if tc == 0 else (), inc=dve)
        m1_at = dve.now
        E.ts("vector", msk[:, :], lt, m1r[:, :], -1e30, ALU.is_equal, ALU.mult,
             waits=[m1_at], inc=dve)
        E.tt("vector", msk[:, :], msk[:, :], lt, ALU.add, inc=dve)
        msk_at = dve.now
        E.red("vector", m2r[:, :], msk[:, :], ALU.max, waits=[msk_at], inc=dve)
        m2_at = dve.now
        E.tt("vector", p1r[:, :], m1r[:, :], m2r[:, :], ALU.subtract,
             waits=[m2_at], inc=dve)
        E.act(p1r[:, :], p1r[:, :], AF.Sigmoid, waits=[dve.now], inc=act)
        sig_at = act.now
        E.ts("vector", p2r[:, :], p1r[:, :], -1.0, 1.0, ALU.mult, ALU.add,
             waits=[sig_at], inc=dve)
        p_at = dve.now
        E.ts("vector", eq1[:, :], lt, m1r[:, :], None, ALU.is_equal,
             waits=[p_at], inc=dve)
        E.ts("vector", eq1[:, :], eq1[:, :], p1r[:, :], None, ALU.mult,
             waits=[dve.now], inc=dve)
        E.ts("vector", wt, msk[:, :], m2r[:, :], None, ALU.is_equal,
             waits=[dve.now], inc=dve)
        E.ts("vector", wt, wt, p2r[:, :], None, ALU.mult, waits=[dve.now],
             inc=dve)
        E.tt("vector", wt, wt, eq1[:, :], ALU.add, inc=dve)
        if getattr(E, "debug_dumps", False):
            E.copy("vector", dbgrow[:, tc * 4:tc * 4 + 1], m1r[:, :], inc=dve)
            E.copy("vector", dbgrow[:, tc * 4 + 1:tc * 4 + 2], m2r[:, :], inc=dve)
            E.copy("vector", dbgrow[:, tc * 4 + 2:tc * 4 + 3], p1r[:, :], inc=dve)
            E.copy("vector", dbgrow[:, tc * 4 + 3:tc * 4 + 4], p2r[:, :], inc=dve)
            E.copy("vector", dbge[:, tc * 16:tc * 16 + 8], eq1[:, :], inc=dve)
            E.copy("vector", dbge[:, tc * 16 + 8:tc * 16 + 16], msk[:, :], inc=dve)
    w_at = dve.now
    E.dbg("rows", dbgrow[:, :], waits=[w_at])
    E.dbg("eqm", dbge[:, :], waits=[w_at])
    E.dbg("LG", LG[0:8, :], waits=[lg_at])
    E.dbg("Ltok", Ltok[:, :], waits=[w_at])
    E.dbg("wtok", wtok[:, :], waits=[w_at])
    for tc in range(NTC):
        pbx = 4 + tc % 2
        E.trans(pb[pbx][0:8, 0:P], wtok[:, tc * 8:(tc + 1) * 8],
                waits=[w_at, tr_war[pbx]], inc=pe,
                ident=E.ident_f32[0:P, 0:P])
        E.copy("vector", wfm8[0:8, tc * P:(tc + 1) * P], pb[pbx][0:8, 0:P],
               waits=[pe.now], inc=dve)
        tr_war[pbx] = dve.now
    E.copy("vector", wfm8b[0:8, :], wfm8[0:8, :], waits=[dve.now], inc=dve)
    wfm_at = dve.now
    ldwr = E.sem("ldwr")
    for e in range(8):
        E.dma("gpsimd", wrows[0:1, e * T:(e + 1) * T], wfm8b[e:e + 1, :], ldwr,
              waits=[wfm_at])
    wrows_at = ldwr.now
    E.dbg("wrows", wrows[0:1, :], waits=[wrows_at])

    wring = getattr(E, "wring", None) or Ring(E, "wrf", [w[:, :] for w in WR])
    x1_written = None
    for e in range(8):
        for th in range(2):
            pbx = 4 + th
            E.mm(pb[pbx][0:P, 0:512], E.ones_row[:, :],
                 wrows[0:1, e * T + th * 512:e * T + (th + 1) * 512],
                 start=True, stop=True,
                 waits=[wrows_at, E.pb_war[pbx]], inc=pe)
            E.copy("vector", WB[:, th * 512:(th + 1) * 512], pb[pbx][0:P, 0:512],
                   waits=[pe.now], inc=dve)
            E.pb_war[pbx] = dve.now
        wb_at = dve.now
        for (wap, dest, fn_act) in ((w1_ap, GB, AF.Silu), (w2_ap, VB, AF.Identity)):
            wap_e = wap[e]
            for fc in range(8):
                slab, slab_at = wslab_load(E, wring, wap_e, fc * P, P)
                last_pe = None
                for th in range(2):
                    pbx = 4 + th
                    for dc in range(NDC):
                        w = [slab_at]
                        if dc == 0:
                            w += [E.pb_war[pbx], ln_at]
                        E.mm(pb[pbx][0:P, 0:512],
                             slab[:, dc * P:(dc + 1) * P],
                             LNF[:, dc * T + th * 512:dc * T + (th + 1) * 512],
                             start=(dc == 0), stop=(dc == NDC - 1), waits=w,
                             inc=pe if dc == NDC - 1 else None)
                    mm_at = pe.now
                    last_pe = mm_at
                    E.act(dest[:, fc * T + th * 512:fc * T + (th + 1) * 512],
                          pb[pbx][0:P, 0:512], fn_act, waits=[mm_at], inc=act)
                    E.pb_war[pbx] = act.now
                wring.set_free(slab, last_pe)
        gv_at = act.now
        for fc in range(8):
            E.tt("vector", GB[:, fc * T:(fc + 1) * T], GB[:, fc * T:(fc + 1) * T],
                 VB[:, fc * T:(fc + 1) * T], ALU.mult,
                 waits=[gv_at, wb_at] if fc == 0 else (), inc=dve)
            E.tt("vector", GB[:, fc * T:(fc + 1) * T], GB[:, fc * T:(fc + 1) * T],
                 WB[:, :], ALU.mult, inc=dve)
        h_at = dve.now
        for oc in range(NDC):
            slab, slab_at = wslab_load(E, wring, w3_ap[e], oc * P, P, nchunk=8)
            last_pe = None
            for th in range(2):
                pbx = 4 + th
                for fc in range(8):
                    w = [slab_at, h_at]
                    if fc == 0:
                        w += [E.pb_war[pbx]]
                    E.mm(pb[pbx][0:P, 0:512], slab[:, fc * P:(fc + 1) * P],
                         GB[:, fc * T + th * 512:fc * T + (th + 1) * 512],
                         start=(fc == 0), stop=(fc == 7), waits=w,
                         inc=pe if fc == 7 else None)
                mm_at = pe.now
                last_pe = mm_at
                xsl = X1[:, oc * T + th * 512:oc * T + (th + 1) * 512]
                E.tt("vector", xsl, xsl, pb[pbx][0:P, 0:512], ALU.add,
                     waits=[mm_at], inc=dve)
                E.pb_war[pbx] = dve.now
            wring.set_free(slab, last_pe)
        x1_written = dve.now
    if x2_out is not None:
        for dc in range(NDC):
            E.dma("sync", x2_out[dc * P:(dc + 1) * P, :],
                  X1[:, dc * T:(dc + 1) * T], sto, waits=[x1_written])
        E.op("sync", lambda e: e.nop(), waits=[sto.now])
    return x1_written


def emit_l1a_part1(E, x2_hbm, wqkvT, phiq_out, kvc_out, kc_out, tot_out,
                   x2_res=None, x2_res_at=None):
    nc = E.nc
    sh = getattr(E, "shared", None)
    if sh is None:
        alloc_consts(E)
        X = nc.alloc_sbuf_tensor("X", [P, NDC * T], F32)
        LNF = nc.alloc_sbuf_tensor("LNF", [P, NDC * T], BF16)
        WR = [nc.alloc_sbuf_tensor(f"wr{i}", [P, 16 * P], BF16) for i in range(2)]
        E.SCR = nc.alloc_sbuf_tensor("scr2", [P, 2 * T], BF16)
        E.RBPR = nc.alloc_sbuf_tensor("rbpr", [P, 2 * T], BF16)
        E.rows_bf = nc.alloc_sbuf_tensor("rows_bf", [1, 2 * T], BF16)
        E.rowm = nc.alloc_sbuf_tensor("rowm", [1, T], BF16)
        phk = nc.alloc_sbuf_tensor("phk", [P, T], BF16)
        vv = nc.alloc_sbuf_tensor("vv", [P, T], BF16)
        kv = nc.alloc_sbuf_tensor("kv", [P, T], BF16)
        sc1 = nc.alloc_sbuf_tensor("sc1", [P, T], F32)
        sc2 = nc.alloc_sbuf_tensor("sc2", [P, T], BF16)
        OUTR = [nc.alloc_sbuf_tensor(f"or{i}", [P, T], F32) for i in range(4)]
        PQR = [nc.alloc_sbuf_tensor(f"pq{i}", [P, T], BF16) for i in range(2)]
        TOT = nc.alloc_sbuf_tensor("TOT", [P, 32], F32)
        S = nc.alloc_psum_tensor("S", [P, 4 * 512], F32)
        pb45 = [nc.alloc_psum_tensor(f"pb{i}", [P, 512], F32) for i in (4, 5)]
        E.pb = [PBView(S, i) for i in range(4)] + [PBView(t, None) for t in pb45]
        E.Sps = S
        E.ln_war = None
        E.pb_war = {4: None, 5: None}
    else:
        X = x2_res
        LNF = sh["A"][:, :]
        WR = sh["WR"]
        Db = sh["D"]
        Df32 = Db[:, :].bitcast(F32)
        phk = Db[:, 0:T]
        vv = Db[:, T:2 * T]
        kv = Db[:, 2 * T:3 * T]
        sc2 = Db[:, 3 * T:4 * T]
        PQR = [Db[:, 4 * T:5 * T], Db[:, 5 * T:6 * T]]
        sc1 = Df32[:, 3 * T:4 * T]
        OUTR = [Df32[:, (4 + i) * T:(5 + i) * T] for i in range(4)]
        TOT = nc.alloc_sbuf_tensor("TOTm", [P, 32], F32)
    pb = E.pb
    pe, dve, act = E.sem("pe"), E.sem("dve"), E.sem("act")
    ldx1 = E.sem("ldx2b")
    spq = [E.sem("spq0"), E.sem("spq1")]
    sor = [E.sem(f"sor{i}") for i in range(4)]
    stt = E.sem("stt")

    if sh is None:
        for dc in range(NDC):
            E.dma("gpsimd", X[:, dc * T:(dc + 1) * T],
                  x2_hbm[dc * P:(dc + 1) * P, :], ldx1)
        x_at = ldx1.now
    else:
        x_at = x2_res_at

    def src(dc):
        def get():
            return X[:, dc * T:(dc + 1) * T], [x_at], (lambda s, c: None)
        return get
    ln_at = emit_ln(E, [src(dc) for dc in range(NDC)], LNF[:, :])

    wring = getattr(E, "wring", None) or Ring(E, "wr", [w[:, :] for w in WR])
    pqring_i = [0]
    pq_freed = [None, None]
    or_i = [0]
    or_freed = [None] * 4
    tot_written = None

    def proj_chunk(wcol, epi):
        slab, slab_at = wslab_load(E, wring, wqkvT, wcol, P)
        last_pe = None
        for th in range(2):
            pbx = 4 + th
            pbb = pb[pbx]
            for dc in range(NDC):
                w = [slab_at]
                if dc == 0:
                    w += [E.pb_war[pbx], ln_at]
                E.mm(pbb.full(), slab[:, dc * P:(dc + 1) * P],
                     LNF[:, dc * T + th * 512:dc * T + (th + 1) * 512],
                     start=(dc == 0), stop=(dc == NDC - 1), waits=w,
                     inc=pe if dc == NDC - 1 else None)
            mm_at = pe.now
            last_pe = mm_at
            epi(th, pbb, mm_at)
        wring.set_free(slab, last_pe)

    def phi_epi(dest):
        def epi(th, pbb, mm_at):
            sl = slice(th * 512, (th + 1) * 512)
            E.ts("vector", sc1[:, sl], pbb.full(), 0.0, None, ALU.min,
                 waits=[mm_at], inc=dve)
            E.act(sc2[:, sl], sc1[:, sl], AF.Exp, waits=[dve.now], inc=act)
            E.stt("vector", dest[:, sl], pbb.full(), 1.0, sc2[:, sl],
                  ALU.add, ALU.max, waits=[act.now, mm_at], inc=dve)
            E.pb_war[4 + th] = dve.now
        return epi

    for j in range(NDC):
        k2 = pqring_i[0] % 2
        pqring_i[0] += 1
        pqv = PQR[k2][:, :]
        if pq_freed[k2] is not None:
            pass
        def q_epi(th, pbb, mm_at, pqv=pqv, k2=k2):
            sl = slice(th * 512, (th + 1) * 512)
            E.ts("vector", sc1[:, sl], pbb.full(), 0.0, None, ALU.min,
                 waits=[mm_at], inc=dve)
            E.act(sc2[:, sl], sc1[:, sl], AF.Exp, waits=[dve.now], inc=act)
            E.stt("vector", pqv[:, sl], pbb.full(), 1.0, sc2[:, sl],
                  ALU.add, ALU.max,
                  waits=[act.now, mm_at] + ([pq_freed[k2]] if th == 0 and pq_freed[k2] else []),
                  inc=dve)
            E.pb_war[4 + th] = dve.now
        proj_chunk(j * P, q_epi)
        E.dma("sync", phiq_out[j * P:(j + 1) * P, :], pqv, spq[k2],
              waits=[dve.now])
        pq_freed[k2] = (spq[k2], spq[k2].n)
        proj_chunk(D + j * P, phi_epi(phk))
        phk_at = dve.now
        def v_epi(th, pbb, mm_at):
            sl = slice(th * 512, (th + 1) * 512)
            E.copy("vector", vv[:, sl], pbb.full(), waits=[mm_at], inc=dve)
            E.pb_war[4 + th] = dve.now
        proj_chunk(2 * D + j * P, v_epi)
        v_at = dve.now
        E.tt("vector", kv[:, :], phk[:, :], vv[:, :], ALU.mult,
             waits=[phk_at, v_at], inc=dve)
        k4 = or_i[0] % 4
        or_i[0] += 1
        kvcv = OUTR[k4][:, :]
        E.scan_cs(kvcv, kv[:, :], 0.0,
                  waits=[dve.now] + ([or_freed[k4]] if or_freed[k4] else []),
                  inc=dve)
        E.copy("vector", TOT[:, j:j + 1], kvcv[:, T - 1:T], inc=dve)
        E.dma("sync", kvc_out[j * P:(j + 1) * P, :], kvcv, sor[k4],
              waits=[dve.now])
        or_freed[k4] = (sor[k4], sor[k4].n)
        k4 = or_i[0] % 4
        or_i[0] += 1
        kcv = OUTR[k4][:, :]
        E.scan_cs(kcv, phk[:, :], 0.0,
                  waits=[dve.now] + ([or_freed[k4]] if or_freed[k4] else []),
                  inc=dve)
        E.copy("vector", TOT[:, 16 + j:17 + j], kcv[:, T - 1:T], inc=dve)
        E.dma("sync", kc_out[j * P:(j + 1) * P, :], kcv, sor[k4],
              waits=[dve.now])
        or_freed[k4] = (sor[k4], sor[k4].n)
        tot_written = dve.now
    E.dma("sync", tot_out, TOT[:, :], stt, waits=[tot_written])
    E.op("sync", lambda e: e.nop(),
         waits=[stt.now, spq[0].now, spq[1].now] + [s.now for s in sor])


def emit_l1a2_l1f(E, x2_hbm, phiq_hbm, kvc_hbm, kc_hbm, ptot_hbm, woutT,
                  wg_ap, wu_ap, wd_ap, sg_pk, su_pk, sd_pk, x4_out):
    nc = E.nc
    alloc_consts(E)
    X = nc.alloc_sbuf_tensor("X", [P, NDC * T], F32)
    LNF = nc.alloc_sbuf_tensor("LNF", [P, NDC * T], BF16)
    OP = LNF
    GU = nc.alloc_sbuf_tensor("GU", [P, 16 * T], BF16)
    WR = [nc.alloc_sbuf_tensor(f"wr{i}", [P, 16 * P], BF16) for i in range(2)]
    E.SCR = nc.alloc_sbuf_tensor("scr2", [P, 2 * T], BF16)
    E.RBPR = nc.alloc_sbuf_tensor("rbpr", [P, 2 * T], BF16)
    E.rows_bf = nc.alloc_sbuf_tensor("rows_bf", [1, 2 * T], BF16)
    E.rowm = nc.alloc_sbuf_tensor("rowm", [1, T], BF16)
    PQR = [nc.alloc_sbuf_tensor(f"pq{i}", [P, T], BF16) for i in range(2)]
    KVR = [nc.alloc_sbuf_tensor(f"kvr{i}", [P, T], F32) for i in range(2)]
    KCR = [nc.alloc_sbuf_tensor(f"kcr{i}", [P, T], F32) for i in range(2)]
    TOT = nc.alloc_sbuf_tensor("TOT", [P, 32], F32)
    sgs = nc.alloc_sbuf_tensor("sgs", [P, 64], F32)
    sus = nc.alloc_sbuf_tensor("sus", [P, 64], F32)
    sds = nc.alloc_sbuf_tensor("sds", [P, 16], F32)
    n1 = nc.alloc_sbuf_tensor("n1", [P, T], F32)
    d1 = nc.alloc_sbuf_tensor("d1", [P, T], F32)
    S = nc.alloc_psum_tensor("S", [P, 4 * 512], F32)
    pb45 = [nc.alloc_psum_tensor(f"pb{i}", [P, 512], F32) for i in (4, 5)]
    E.pb = [PBView(S, i) for i in range(4)] + [PBView(t, None) for t in pb45]
    pb = E.pb
    E.ln_war = None
    E.pb_war = {4: None, 5: None}
    pe, dve, act = E.sem("pe"), E.sem("dve"), E.sem("act")
    ldx1, ldsc, sto = E.sem("ldx1"), E.sem("ldsc"), E.sem("sto")

    for dc in range(NDC):
        E.dma("gpsimd", X[:, dc * T:(dc + 1) * T], x2_hbm[dc * P:(dc + 1) * P, :],
              ldx1)
    E.dma("gpsimd", TOT[:, :], ptot_hbm, ldx1)
    E.dma("gpsimd", sgs[:, :], sg_pk, ldsc)
    E.dma("gpsimd", sus[:, :], su_pk, ldsc)
    E.dma("gpsimd", sds[:, :], sd_pk, ldsc)
    x_at = ldx1.now
    sc_at = ldsc.now

    pqring = Ring(E, "pqr", [t[:, :] for t in PQR])
    kvring = Ring(E, "kvr", [t[:, :] for t in KVR])
    kcring = Ring(E, "kcr", [t[:, :] for t in KCR])
    op_at = None
    for j in range(NDC):
        pqv, pq_at = pqring.load("gpsimd", phiq_hbm[j * P:(j + 1) * P, :])
        kvv, kv_at = kvring.load("gpsimd", kvc_hbm[j * P:(j + 1) * P, :])
        kcv, kc_at = kcring.load("gpsimd", kc_hbm[j * P:(j + 1) * P, :])
        E.ts("vector", n1[:, :], kvv, TOT[:, j:j + 1], None, ALU.add,
             waits=[kv_at, x_at], inc=dve)
        E.tt("vector", n1[:, :], n1[:, :], pqv, ALU.mult, waits=[pq_at], inc=dve)
        E.ts("vector", d1[:, :], kcv, TOT[:, 16 + j:17 + j], None, ALU.add,
             waits=[kc_at], inc=dve)
        E.tt("vector", d1[:, :], d1[:, :], pqv, ALU.mult, inc=dve)
        E.ts("vector", d1[:, :], d1[:, :], 1e-9, None, ALU.add, inc=dve)
        E.recip(d1[:, :], d1[:, :], inc=dve)
        E.tt("vector", LNF[:, j * T:(j + 1) * T], n1[:, :], d1[:, :], ALU.mult,
             inc=dve)
        op_at = dve.now
        pqring.set_free(pqv, op_at)
        kvring.set_free(kvv, op_at)
        kcring.set_free(kcv, op_at)
    wring = Ring(E, "wr", [w[:, :] for w in WR])
    RS128 = float(1.0 / np.sqrt(np.float64(128.0)))
    for oc in range(NDC):
        slab, slab_at = wslab_load(E, wring, woutT, oc * P, P)
        last_pe = None
        for th in range(2):
            pbx = 4 + th
            pbb = pb[pbx]
            for dc in range(NDC):
                w = [slab_at]
                if dc == 0:
                    w += [E.pb_war[pbx], op_at]
                E.mm(pbb.full(), slab[:, dc * P:(dc + 1) * P],
                     LNF[:, dc * T + th * 512:dc * T + (th + 1) * 512],
                     start=(dc == 0), stop=(dc == NDC - 1), waits=w,
                     inc=pe if dc == NDC - 1 else None)
            mm_at = pe.now
            last_pe = mm_at
            xsl = X[:, oc * T + th * 512:oc * T + (th + 1) * 512]
            E.stt("vector", xsl, pbb.full(), RS128, xsl, ALU.mult, ALU.add,
                  waits=[mm_at, x_at], inc=dve)
            E.pb_war[pbx] = dve.now
        wring.set_free(slab, last_pe)
    x3_at = dve.now

    def src(dc):
        def get():
            return X[:, dc * T:(dc + 1) * T], [x3_at], (lambda s, c: None)
        return get
    ln_at = emit_ln(E, [src(dc) for dc in range(NDC)], LNF[:, :])

    for blk in range(8):
        for half, (wap, ss, fn) in enumerate(((wg_ap, sgs, AF.Silu),
                                              (wu_ap, sus, AF.Identity))):
            for fc in range(8):
                ic = blk * 8 + fc
                slab, slab_at = wslab_load(E, wring, wap, ic * P, P)
                last_pe = None
                for th in range(2):
                    pbx = 4 + th
                    pbb = pb[pbx]
                    for dc in range(NDC):
                        w = [slab_at]
                        if dc == 0:
                            w += [E.pb_war[pbx], ln_at]
                        E.mm(pbb.full(), slab[:, dc * P:(dc + 1) * P],
                             LNF[:, dc * T + th * 512:dc * T + (th + 1) * 512],
                             start=(dc == 0), stop=(dc == NDC - 1), waits=w,
                             inc=pe if dc == NDC - 1 else None)
                    mm_at = pe.now
                    last_pe = mm_at
                    dst = GU[:, half * 8 * T + fc * T + th * 512:
                             half * 8 * T + fc * T + (th + 1) * 512]
                    E.act(dst, pbb.full(), fn, scale=ss[:, ic:ic + 1],
                          waits=[mm_at, sc_at], inc=act)
                    E.pb_war[pbx] = act.now
                wring.set_free(slab, last_pe)
        gu_at = act.now
        for fc in range(8):
            E.tt("vector", GU[:, fc * T:(fc + 1) * T], GU[:, fc * T:(fc + 1) * T],
                 GU[:, 8 * T + fc * T:8 * T + (fc + 1) * T], ALU.mult,
                 waits=[gu_at] if fc == 0 else (), inc=dve)
        fuse_at = dve.now
        for oc in range(NDC):
            slab, slab_at = wslab_load(E, wring, wd_ap[blk * 1024:(blk + 1) * 1024, :],
                                       oc * P, P, nchunk=8)
            last_pe = None
            for th in range(2):
                pbx = 4 + th
                pbb = pb[pbx]
                for fc in range(8):
                    w = [slab_at, fuse_at]
                    if fc == 0:
                        w += [E.pb_war[pbx]]
                    E.mm(pbb.full(), slab[:, fc * P:(fc + 1) * P],
                         GU[:, fc * T + th * 512:fc * T + (th + 1) * 512],
                         start=(fc == 0), stop=(fc == 7), waits=w,
                         inc=pe if fc == 7 else None)
                mm_at = pe.now
                last_pe = mm_at
                xsl = X[:, oc * T + th * 512:oc * T + (th + 1) * 512]
                E.stt("vector", xsl, pbb.full(), sds[:, oc:oc + 1], xsl,
                      ALU.mult, ALU.add, waits=[mm_at, sc_at], inc=dve)
                E.pb_war[pbx] = dve.now
            wring.set_free(slab, last_pe)
    x4_at = dve.now
    for dc in range(NDC):
        E.dma("sync", x4_out[dc * P:(dc + 1) * P, :], X[:, dc * T:(dc + 1) * T],
              sto, waits=[x4_at])
    E.op("sync", lambda e: e.nop(), waits=[sto.now])


def emit_merged_A(E, xo, xp, wqkvT0, bqpk, bvrow, woutT0, boutpk, rwT,
                  w1_ap, w2_ap, w3_ap, l1qkvT, x2_out, phiq_out, kvc_out,
                  kc_out, tot_out):
    nc = E.nc
    alloc_consts(E)
    A = nc.alloc_sbuf_tensor("regA", [P, NDC * T], BF16)
    B = nc.alloc_sbuf_tensor("regB", [P, NDC * T], BF16)
    C = nc.alloc_sbuf_tensor("regC", [P, NDC * 2 * T], BF16)
    Dq = nc.alloc_sbuf_tensor("regD", [P, NDC * T], BF16)
    WR = [nc.alloc_sbuf_tensor(f"wr{i}", [P, NDC * 256], BF16) for i in range(2)]
    XR = [nc.alloc_sbuf_tensor(f"xr{i}", [P, T], F32) for i in range(2)]
    E.SCR = nc.alloc_sbuf_tensor("scr2", [P, 2 * T], BF16)
    E.RBPR = nc.alloc_sbuf_tensor("rbpr", [P, 2 * T], BF16)
    E.rows_bf = nc.alloc_sbuf_tensor("rows_bf", [1, 2 * T], BF16)
    E.rowm = nc.alloc_sbuf_tensor("rowm", [1, T], BF16)
    S = nc.alloc_psum_tensor("S", [P, 4 * 512], F32)
    pb45 = [nc.alloc_psum_tensor(f"pb{i}", [P, 512], F32) for i in (4, 5)]
    E.pb = [PBView(S, i) for i in range(4)] + [PBView(t, None) for t in pb45]
    E.Sps = S
    E.ptb = [nc.alloc_psum_tensor(f"ptb{i}", [P, P], BF16) for i in range(2)]
    E.ln_war = None
    E.pb_war = {4: None, 5: None}
    E.shared = {"A": A, "B": B, "C": C, "D": Dq, "WR": WR, "XR": XR}
    E.wring = Ring(E, "wsh", [w[:, :] for w in WR])
    CX = C[:, :].bitcast(F32)

    x1_at = emit_l0_attn(E, xo, xp, wqkvT0, bqpk, bvrow, woutT0, boutpk,
                         x1_out=None, x1_res=CX)
    x2_at = emit_l0_ffn(E, None, rwT, w1_ap, w2_ap, w3_ap, None,
                        x1_res=CX, x1_res_at=x1_at)
    emit_l1a_part1(E, None, l1qkvT, phiq_out, kvc_out, kc_out, tot_out,
                   x2_res=CX, x2_res_at=x2_at)
    stox = E.sem("stox")
    for dc in range(NDC):
        E.dma("sync", x2_out[dc * P:(dc + 1) * P, :], CX[:, dc * T:(dc + 1) * T],
              sto_sem := stox, waits=[x2_at])
    E.op("sync", lambda e: e.nop(), waits=[stox.now])


from concourse.bass_utils import run_bass_kernel_spmd


def _build_A():
    nc = bass.Bass("TRN2", target_bir_lowering=False, debug=False,
                   detect_race_conditions=False)
    xo = nc.dram_tensor("xo", [D, T], F32, kind="ExternalInput").ap()
    xp = nc.dram_tensor("xp", [D, T], F32, kind="ExternalInput").ap()
    wqkvT0 = nc.dram_tensor("wqkvT0", [D, 3 * D], F32, kind="ExternalInput").ap()
    woutT0 = nc.dram_tensor("woutT0", [D, D], F32, kind="ExternalInput").ap()
    bqpk = nc.dram_tensor("bqpk", [P, 48], F32, kind="ExternalInput").ap()
    bvrow = nc.dram_tensor("bvrow", [1, D], F32, kind="ExternalInput").ap()
    boutpk = nc.dram_tensor("boutpk", [P, 16], F32, kind="ExternalInput").ap()
    rwT = nc.dram_tensor("rwT", [D, 8], F32, kind="ExternalInput").ap()
    w1 = nc.dram_tensor("w1", [8, D, 1024], F32, kind="ExternalInput")
    w2 = nc.dram_tensor("w2", [8, D, 1024], F32, kind="ExternalInput")
    w3 = nc.dram_tensor("w3", [8, 1024, D], F32, kind="ExternalInput")
    l1qkvT = nc.dram_tensor("l1qkvT", [D, 3 * D], F32, kind="ExternalInput").ap()
    x2o = nc.dram_tensor("x2o", [D, T], F32, kind="ExternalOutput").ap()
    phiq = nc.dram_tensor("phiq", [D, T], BF16, kind="ExternalOutput").ap()
    kvc = nc.dram_tensor("kvc", [D, T], F32, kind="ExternalOutput").ap()
    kc = nc.dram_tensor("kc", [D, T], F32, kind="ExternalOutput").ap()
    tot = nc.dram_tensor("tot", [P, 32], F32, kind="ExternalOutput").ap()
    E = Emit(nc)
    emit_merged_A(E, xo, xp, wqkvT0, bqpk, bvrow, woutT0, boutpk, rwT,
                  w1.ap(), w2.ap(), w3.ap(), l1qkvT, x2o, phiq, kvc, kc, tot)
    E.build()
    return nc


def _build_B():
    nc = bass.Bass("TRN2", target_bir_lowering=False, debug=False,
                   detect_race_conditions=False)
    x2 = nc.dram_tensor("x2", [D, T], F32, kind="ExternalInput").ap()
    phiq = nc.dram_tensor("phiq", [D, T], BF16, kind="ExternalInput").ap()
    kvc = nc.dram_tensor("kvc", [D, T], F32, kind="ExternalInput").ap()
    kc = nc.dram_tensor("kc", [D, T], F32, kind="ExternalInput").ap()
    ptot = nc.dram_tensor("ptot", [P, 32], F32, kind="ExternalInput").ap()
    woutT = nc.dram_tensor("woutT", [D, D], F32, kind="ExternalInput").ap()
    wg = nc.dram_tensor("wg", [D, 8192], I8, kind="ExternalInput").ap()
    wu = nc.dram_tensor("wu", [D, 8192], I8, kind="ExternalInput").ap()
    wd = nc.dram_tensor("wd", [8192, D], I8, kind="ExternalInput").ap()
    sg = nc.dram_tensor("sg", [P, 64], F32, kind="ExternalInput").ap()
    su = nc.dram_tensor("su", [P, 64], F32, kind="ExternalInput").ap()
    sd = nc.dram_tensor("sd", [P, 16], F32, kind="ExternalInput").ap()
    x4 = nc.dram_tensor("x4", [D, T], F32, kind="ExternalOutput").ap()
    E = Emit(nc)
    emit_l1a2_l1f(E, x2, phiq, kvc, kc, ptot, woutT, wg, wu, wd, sg, su, sd, x4)
    E.build()
    return nc


def kernel(L0_attn_hidden_states, L0_attn_w_qkv, L0_attn_b_qkv, L0_attn_w_out,
           L0_attn_b_out, L0_ffn_router_w, L0_ffn_w1, L0_ffn_w2, L0_ffn_w3,
           L1_attn_W_qkv, L1_attn_W_out, L1_ffn_w_gate, L1_ffn_w_up,
           L1_ffn_w_down, L1_ffn_s_gate, L1_ffn_s_up, L1_ffn_s_down):
    x = np.ascontiguousarray(np.asarray(L0_attn_hidden_states, np.float32))
    bq = np.asarray(L0_attn_b_qkv, np.float32)
    base_A = {
        "wqkvT0": np.ascontiguousarray(np.asarray(L0_attn_w_qkv, np.float32).T),
        "woutT0": np.ascontiguousarray(np.asarray(L0_attn_w_out, np.float32).T),
        "bqpk": np.ascontiguousarray(bq.reshape(48, P).T),
        "bvrow": np.ascontiguousarray(bq[2 * D:3 * D].reshape(1, D)),
        "boutpk": np.ascontiguousarray(
            np.asarray(L0_attn_b_out, np.float32).reshape(16, P).T),
        "rwT": np.ascontiguousarray(np.asarray(L0_ffn_router_w, np.float32).T),
        "w1": np.ascontiguousarray(np.asarray(L0_ffn_w1, np.float32)),
        "w2": np.ascontiguousarray(np.asarray(L0_ffn_w2, np.float32)),
        "w3": np.ascontiguousarray(np.asarray(L0_ffn_w3, np.float32)),
        "l1qkvT": np.ascontiguousarray(np.asarray(L1_attn_W_qkv, np.float32).T),
    }
    base_B = {
        "woutT": np.ascontiguousarray(np.asarray(L1_attn_W_out, np.float32).T),
        "wg": np.ascontiguousarray(np.asarray(L1_ffn_w_gate, np.int8)),
        "wu": np.ascontiguousarray(np.asarray(L1_ffn_w_up, np.int8)),
        "wd": np.ascontiguousarray(np.asarray(L1_ffn_w_down, np.int8)),
        "sg": np.ascontiguousarray(
            np.asarray(L1_ffn_s_gate, np.float32).reshape(64, P).T),
        "su": np.ascontiguousarray(
            np.asarray(L1_ffn_s_up, np.float32).reshape(64, P).T),
        "sd": np.ascontiguousarray(
            np.asarray(L1_ffn_s_down, np.float32).reshape(16, P).T),
    }
    zt = np.zeros((P, 32), np.float32)
    xT = [np.ascontiguousarray(x[b].T) for b in range(4)]
    cores = list(range(8))
    inA = []
    for c in cores:
        b, h = c // 2, c % 2
        d = dict(base_A)
        d["xo"] = xT[b][:, h * T:(h + 1) * T]
        d["xp"] = xT[b][:, (1 - h) * T:(2 - h) * T]
        inA.append(d)
    rA = run_bass_kernel_spmd(_build_A(), inA, core_ids=cores)
    inB = []
    for c in cores:
        b, h = c // 2, c % 2
        ptot = zt if h == 0 else np.asarray(rA.results[2 * b]["tot"], np.float32)
        d = dict(base_B)
        d.update({"x2": np.asarray(rA.results[c]["x2o"]),
                  "phiq": np.asarray(rA.results[c]["phiq"]),
                  "kvc": np.asarray(rA.results[c]["kvc"], np.float32),
                  "kc": np.asarray(rA.results[c]["kc"], np.float32),
                  "ptot": ptot})
        inB.append(d)
    rB = run_bass_kernel_spmd(_build_B(), inB, core_ids=cores)
    out = np.empty((4, 2 * T, D), np.float32)
    for c in cores:
        b, h = c // 2, c % 2
        out[b, h * T:(h + 1) * T, :] = np.asarray(rB.results[c]["x4"],
                                                  np.float32).T
    return out
